# version 34
# baseline (speedup 1.0000x reference)
import numpy as np
from contextlib import ExitStack

import concourse.bass as bass
import concourse.mybir as mybir
from concourse.bass_utils import run_bass_kernel_spmd

F32 = mybir.dt.float32
BF16 = mybir.dt.bfloat16
ALU = mybir.AluOpType
ACT = mybir.ActivationFunctionType
AX = mybir.AxisListType

NCORES = 8
B, SEQ, D = 16, 2048, 1024
H, DH = 8, 64
VD = 512
FF = 2816
BPC = B // NCORES
TOK = BPC * SEQ
RS2 = float(2.0 ** -0.5)
LN_EPS = 1e-5
QTHRESH = [0.25, 0.75, 1.25, 1.75, 2.5, 3.5, 5.0]
EXPSHIFT = -20.0
ATTN_SCALE = 0.125
NSLICES = [(0, 1024), (1024, 896), (1920, 896)]

SLOT = {
    "xt": (0, 1024), "xn": (1024, 2048), "ev": (2048, 2560), "evb": (2560, 3072),
    "xnb": (3072, 4096), "xnb2": (4096, 5120),
    "exp0": (0, 512), "exp1": (512, 1024), "y": (1024, 2048),
    "xnt": (2048, 3072), "sout": (3072, 4096),
    "exp0b": (4096, 4608), "exp1b": (4608, 5120), "bot": (5120, 6144),
    "h": (0, 1024), "sg": (1024, 2048), "gp": (2048, 3072),
    "xres": (3072, 4096), "sres": (4096, 5120), "hb": (5120, 6144),
}
BSLOT = {
    "xqT": (0, 2048), "xqTb": (2048, 4096), "ebA": (4096, 5120),
    "ebB": (5120, 6144),
    "ph0": (0, 1024), "ph1": (1024, 2048), "ph2": (2048, 3072),
    "ph3": (3072, 4096),
}


def bslot_keys(name):
    lo, hi = BSLOT[name]
    return ["sb%d" % g for g in range(lo // 512, (hi + 511) // 512)]


def slot_keys(name):
    lo, hi = SLOT[name]
    g0, g1 = lo // 512, (hi + 511) // 512
    return ["stg%d" % g for g in range(g0, g1)]


class Sched:

    ENG_SEM = {"tensor": "T", "vector": "V", "scalar": "S", "gpsimd": "G", "sync": None}

    def __init__(self):
        self.items = []
        self.counts = {k: 0 for k in ("T", "V", "S", "G")}
        self.last_w = {}
        self.readers = {}
        self.marks = {}
        self.sems = None

    def count_of(self, sem):
        return self.counts.setdefault(sem, 0)

    def _wait_list(self, eng, reads, writes):
        need = {}
        for k in reads:
            w = self.last_w.get(k)
            if w:
                need[w[0]] = max(need.get(w[0], 0), w[1])
        for k in writes:
            w = self.last_w.get(k)
            if w:
                need[w[0]] = max(need.get(w[0], 0), w[1])
            for sem, cnt in self.readers.get(k, {}).items():
                need[sem] = max(need.get(sem, 0), cnt)
        out = []
        for sem, cnt in sorted(need.items()):
            if eng == "tensor" and sem == "T":
                continue
            mk = (eng, sem)
            if self.marks.get(mk, -1) >= cnt:
                continue
            out.append((sem, cnt))
        return out

    def op(self, eng, fn, reads=(), writes=(), dma=None):
        reads, writes = list(reads), list(writes)
        waits = self._wait_list(eng, reads, writes)
        sem = dma if dma else self.ENG_SEM[eng]
        amt = 16 if dma else 1
        if dma:
            prev = self.count_of(sem)
            if prev > 0 and self.marks.get((eng, sem), -1) < prev:
                waits = waits + [(sem, prev)]
        for sem_, cnt_ in waits:
            self.marks[(eng, sem_)] = cnt_
        self.counts[sem] = self.count_of(sem) + amt
        cnt_after = self.counts[sem]

        def closure(engine, _w=tuple(waits), _fn=fn, _sem=sem, _amt=amt):
            for s_, c_ in _w:
                engine.wait_ge(self.sems[s_], c_)
            instr = _fn(engine)
            if instr is not None:
                instr.then_inc(self.sems[_sem], _amt)

        self.items.append((eng, closure))
        for k in reads:
            self.readers.setdefault(k, {})[sem] = cnt_after
        for k in writes:
            self.last_w[k] = (sem, cnt_after)
            self.readers[k] = {}
        return cnt_after

    def raw(self, eng, fn):
        self.items.append((eng, lambda engine, _fn=fn: _fn(engine)))

    def replay(self, eng_name):
        def f(engine):
            for e_, closure in self.items:
                if e_ == eng_name:
                    closure(engine)
        return f


def build(dbg=None):
    nc = bass.Bass("TRN2", target_bir_lowering=False, detect_race_conditions=False)

    x_d = nc.dram_tensor("x", [BPC, SEQ, D], F32, kind="ExternalInput").ap()
    wqk_d = nc.dram_tensor("wqk", [D, D], F32, kind="ExternalInput").ap()
    bqk_d = nc.dram_tensor("bqk", [D], F32, kind="ExternalInput").ap()
    wv_d = nc.dram_tensor("wv", [D, VD], F32, kind="ExternalInput").ap()
    bv_d = nc.dram_tensor("bv", [VD], F32, kind="ExternalInput").ap()
    wo_d = nc.dram_tensor("wo", [VD, D], F32, kind="ExternalInput").ap()
    bo_d = nc.dram_tensor("bo", [D], F32, kind="ExternalInput").ap()
    gam_d = nc.dram_tensor("gam", [D], F32, kind="ExternalInput").ap()
    bet_d = nc.dram_tensor("bet", [D], F32, kind="ExternalInput").ap()
    wg_d = nc.dram_tensor("wg", [FF, D], F32, kind="ExternalInput").ap()
    wu_d = nc.dram_tensor("wu", [FF, D], F32, kind="ExternalInput").ap()
    wd_d = nc.dram_tensor("wd", [D, FF], F32, kind="ExternalInput").ap()
    out_d = nc.dram_tensor("out", [BPC, SEQ, D], F32, kind="ExternalOutput").ap()

    def scratch(name, shape, dt=F32):
        kind = "Internal" if dbg is None else "ExternalOutput"
        return nc.dram_tensor(name, shape, dt, kind=kind).ap()

    d_xn = scratch("d_xn", [TOK, D])
    d_xqT = scratch("d_xqT", [2, TOK // 128, 128, 8 * 128], BF16)
    d_q = scratch("d_q", [2, BPC, 4, 128, SEQ], BF16)
    d_k = scratch("d_k", [2, BPC, 4, 128, SEQ], BF16)
    d_v = scratch("d_v", [2, BPC, SEQ, VD], BF16)
    d_s = scratch("d_s", [TOK, D])
    d_g = scratch("d_g", [TOK, D])
    d_r = scratch("d_r", [8, 512])
    d_r2 = scratch("d_r2", [8, 512])
    d_h = scratch("d_h", [TOK, 896]) if dbg == "full" else None

    x_f = x_d.rearrange("s t d -> (s t) d")
    out_f = out_d.rearrange("s t d -> (s t) d")

    sch = Sched()

    with ExitStack() as ES:
        e = ES.enter_context
        wAb = e(nc.sbuf_tensor("wAb", [128, 16384], BF16))
        wB = e(nc.sbuf_tensor("wB", [128, 16384], BF16))
        wC = e(nc.sbuf_tensor("wC", [128, 16640], BF16))
        wDb = e(nc.sbuf_tensor("wDb", [128, 8192], BF16))
        xnTb = e(nc.sbuf_tensor("xnTb", [128, 8192], BF16))
        stage = e(nc.sbuf_tensor("stage", [128, 6144], F32))
        stgb = e(nc.sbuf_tensor("stgb", [128, 6144], BF16))
        qb = e(nc.sbuf_tensor("qb", [128, 4096], BF16))
        ob = e(nc.sbuf_tensor("ob", [128, 2048], F32))
        qs1 = e(nc.sbuf_tensor("qs1", [128, 1024], F32))
        qs2 = e(nc.sbuf_tensor("qs2", [128, 1024], F32))
        qs3 = e(nc.sbuf_tensor("qs3", [128, 1024], F32))
        qs4 = e(nc.sbuf_tensor("qs4", [128, 1024], F32))
        qsc = e(nc.sbuf_tensor("qsc", [128, 64], F32))
        qrs = e(nc.sbuf_tensor("qrs", [128, 64], F32))
        bn6 = e(nc.sbuf_tensor("bn6", [128, 12], F32))
        bn2 = e(nc.sbuf_tensor("bn2", [128, 2], F32))
        sqv = e(nc.sbuf_tensor("sqv", [128, 1], F32))
        rstd = e(nc.sbuf_tensor("rstd", [128, 1], F32))
        bq = e(nc.sbuf_tensor("bq", [128, 4], F32))
        bk = e(nc.sbuf_tensor("bk", [128, 4], F32))
        bvt = e(nc.sbuf_tensor("bvt", [128, 512], F32))
        cst = e(nc.sbuf_tensor("cst", [128, 4], F32))
        qsb = e(nc.sbuf_tensor("qsb", [128, 2048], BF16))
        vsb = e(nc.sbuf_tensor("vsb", [128, 1024], BF16))
        identb = e(nc.sbuf_tensor("identb", [128, 128], BF16))
        sums = e(nc.sbuf_tensor("sums", [128, 512], F32))
        pA = e(nc.psum_tensor("pA", [128, 512], F32))
        pB = e(nc.psum_tensor("pB", [128, 512], F32))
        pC = e(nc.psum_tensor("pC", [128, 512], F32))
        pD = e(nc.psum_tensor("pD", [128, 512], F32))
        pE = e(nc.psum_tensor("pE", [128, 512], F32))
        pF = e(nc.psum_tensor("pF", [128, 512], F32))
        pT1 = e(nc.psum_tensor("pT1", [128, 4, 256], BF16))
        pT2 = e(nc.psum_tensor("pT2", [128, 4, 256], BF16))

        _channels = ("T", "V", "S", "G",
                     "Lxt", "Lxnt", "Lgp", "Lxres", "Lsres", "LwA", "LwB",
                     "LwC", "LwD", "LxnT", "Lqb", "Lrs", "Lrb", "Lbias",
                     "Oxn", "OxqT", "Oq", "Ok", "Ov", "Os", "Og", "Oout",
                     "Or", "Or2", "Oh")
        sems = {s_: e(nc.semaphore("sem" + s_)) for s_ in _channels}
        sch.sems = sems
        block = e(nc.Block())

        def st(slot, lo=None, hi=None):
            a, b_ = SLOT[slot]
            if lo is not None:
                a, b_ = a + lo, a + hi
            return stage[:, a:b_]

        def dma(queue, out_ap, in_ap, reads, writes, ch, slow=False):
            sch.op(queue,
                   lambda g, o=out_ap, i=in_ap, s_=slow:
                       g.dma_start(out=o, in_=i, allow_slow_non_contiguous=s_),
                   reads=reads, writes=writes, dma=ch)

        def load(out_ap, in_ap, writes, ch, reads=()):
            dma("sync", out_ap, in_ap, reads, writes, ch)

        def store(out_ap, in_ap, reads, ch, writes=()):
            dma("scalar", out_ap, in_ap, reads, writes, ch)

        def vop(fn, reads, writes):
            sch.op("vector", fn, reads, writes)

        def sop(fn, reads, writes):
            sch.op("scalar", fn, reads, writes)

        def top(fn, reads, writes):
            sch.op("tensor", fn, reads, writes)

        def gop(fn, reads, writes):
            sch.op("gpsimd", fn, reads, writes)

        gop(lambda g: g.memset(identb[:], 0.0), [], ["identb"])
        gop(lambda g: g.affine_select(
                out=identb[:], in_=identb[:], compare_op=ALU.not_equal,
                fill=1.0, base=0, pattern=[[-1, 128]], channel_multiplier=1),
            ["identb"], ["identb"])

        for i_, val in enumerate([LN_EPS, -4.0, -6.0, EXPSHIFT]):
            gop(lambda g, i_=i_, val=val: g.memset(cst[:, i_:i_ + 1], val),
                [], ["cst"])
        bqk_r = bqk_d.rearrange("(hp hd d par) -> (hd d) hp par", hp=4, hd=2, d=64, par=2)
        dma("sync", bq[:], bqk_r[:, :, 0], [], ["bq"], "Lbias", slow=True)
        dma("sync", bk[:], bqk_r[:, :, 1], [], ["bk"], "Lbias", slow=True)
        load(bvt[:], bv_d.partition_broadcast(128), ["bvt"], "Lbias")

        def bst(slot):
            lo, hi = BSLOT[slot]
            return stgb[:, lo:hi]

        def quantize_ops(src_ap, rkeys, F, SC, split=False, allv=False):
            q1, K1 = SC["q1"]
            q2, K2 = SC["q2"]
            q3, K3 = SC["q3"]
            scb, KSC = SC["sc"]
            rsb, KRS = SC["rs"]
            sb, KSB = SC["sb"]
            nb = F // 16
            src3 = src_ap.rearrange("p (nb bl) -> p nb bl", bl=16)
            ops = []
            E = ops.append
            E(lambda: vop(lambda v: v.tensor_reduce(scb[:, :nb], src3, axis=AX.X,
                                                    op=ALU.max,
                                                    apply_absolute_value=True),
                          rkeys, KSC))
            if allv:
                E(lambda: vop(lambda v: v.scalar_tensor_tensor(
                        q1[:, :F], src_ap, -1.0, src_ap,
                        op0=ALU.mult, op1=ALU.max), rkeys, K1))
            else:
                E(lambda: sop(lambda s: s.activation(out=q1[:, :F], in_=src_ap,
                                                     func=ACT.Abs), rkeys, K1))
            E(lambda: vop(lambda v: v.tensor_scalar(scb[:, :nb], scb[:, :nb],
                                                    1.0 / 6.0, 1e-12,
                                                    op0=ALU.mult, op1=ALU.max),
                          KSC, KSC))
            E(lambda: vop(lambda v: v.reciprocal(rsb[:, :nb], scb[:, :nb]),
                          KSC, KRS))
            rs_b = rsb[:, :nb].unsqueeze(2).broadcast_to([128, nb, 16])
            E(lambda: vop(lambda v: v.tensor_tensor(
                    q2[:, :F].rearrange("p (nb bl) -> p nb bl", bl=16),
                    q1[:, :F].rearrange("p (nb bl) -> p nb bl", bl=16),
                    rs_b, op=ALU.mult),
                K1 + KRS, K2))
            E(lambda: vop(lambda v: v.tensor_scalar(q3[:, :F], q2[:, :F],
                                                    QTHRESH[0], None,
                                                    op0=ALU.is_gt),
                          K2, K3))
            cnt = [(q3, K3), (q1, K1)]
            for j, c in enumerate(QTHRESH[1:]):
                a, ka = cnt[j % 2]
                o, ko = cnt[(j + 1) % 2]
                E(lambda c=c, a=a, o=o, ka=ka, ko=ko: vop(
                    lambda v: v.scalar_tensor_tensor(o[:, :F], q2[:, :F], c,
                                                     a[:, :F],
                                                     op0=ALU.is_gt, op1=ALU.add),
                    K2 + ka, ko))
            kk, kkk = cnt[0]
            if allv:
                E(lambda: vop(lambda v: v.tensor_scalar(q1[:, :F], kk[:, :F],
                                                        4.0, 0.0,
                                                        op0=ALU.subtract,
                                                        op1=ALU.max), kkk, K1))
                E(lambda: vop(lambda v: v.tensor_scalar(q2[:, :F], kk[:, :F],
                                                        6.0, 0.0,
                                                        op0=ALU.subtract,
                                                        op1=ALU.max), kkk, K2))
            else:
                E(lambda: sop(lambda s: s.activation(out=q1[:, :F], in_=kk[:, :F],
                                                     func=ACT.Relu,
                                                     bias=cst[:, 1:2]),
                              kkk + ["cst"], K1))
                E(lambda: sop(lambda s: s.activation(out=q2[:, :F], in_=kk[:, :F],
                                                     func=ACT.Relu,
                                                     bias=cst[:, 2:3]),
                              kkk + ["cst"], K2))
            E(lambda: vop(lambda v: v.scalar_tensor_tensor(q1[:, :F], q1[:, :F],
                                                           0.5, q2[:, :F],
                                                           op0=ALU.mult,
                                                           op1=ALU.add),
                          K1 + K2, K1))
            E(lambda: vop(lambda v: v.scalar_tensor_tensor(q1[:, :F], kk[:, :F],
                                                           0.5, q1[:, :F],
                                                           op0=ALU.mult,
                                                           op1=ALU.add),
                          kkk + K1, K1))
            if allv:
                E(lambda: vop(lambda v: v.tensor_scalar(q2[:, :F], src_ap,
                                                        0.0, None,
                                                        op0=ALU.is_gt),
                              rkeys, K2))
                E(lambda: vop(lambda v: v.tensor_scalar(q2[:, :F], q2[:, :F],
                                                        2.0, -1.0,
                                                        op0=ALU.mult,
                                                        op1=ALU.add), K2, K2))
            else:
                E(lambda: sop(lambda s: s.activation(out=q2[:, :F], in_=src_ap,
                                                     func=ACT.Sign), rkeys, K2))
            E(lambda: vop(lambda v: v.tensor_tensor(q1[:, :F], q1[:, :F],
                                                    q2[:, :F], op=ALU.mult),
                          K1 + K2, K1))
            sc_b = scb[:, :nb].unsqueeze(2).broadcast_to([128, nb, 16])
            E(lambda: vop(lambda v: v.tensor_tensor(
                    src3, q1[:, :F].rearrange("p (nb bl) -> p nb bl", bl=16),
                    sc_b, op=ALU.mult),
                K1 + KSC, rkeys))
            if split:
                if allv:
                    E(lambda: vop(lambda v: v.tensor_copy(sb[:, 0:F], src_ap),
                                  rkeys, KSB))
                else:
                    E(lambda: sop(lambda s: s.activation(out=sb[:, 0:F],
                                                         in_=src_ap,
                                                         func=ACT.Identity),
                                  rkeys, KSB))
                E(lambda: vop(lambda v: v.scalar_tensor_tensor(
                        sb[:, 1024:1024 + F], src_ap, 1.0, sb[:, 0:F],
                        op0=ALU.mult, op1=ALU.subtract),
                    rkeys + KSB, KSB))
            return ops

        QSET0 = {"q1": (qs1, ["qs1"]), "q2": (qs2, ["qs2"]),
                 "q3": (qs3, ["qs3"]),
                 "sc": (qsc, ["qsc"]), "rs": (qrs, ["qrs"]),
                 "sb": (qsb, ["qsb"])}

        def quantize(src_ap, rkeys, F, split=False, allv=False):
            for f_ in quantize_ops(src_ap, rkeys, F, QSET0, split, allv):
                f_()

        pT1b = pT1[:]
        pT2b = pT2[:]

        def transpose_tile_b(src_b, rkeys, nchunks, dst_fn, dst_keys):
            for grp in range((nchunks + 3) // 4):
                bank, bname = (pT1b, "pT1") if grp % 2 == 0 else (pT2b, "pT2")
                n_in = min(4, nchunks - grp * 4)
                for j in range(n_in):
                    fc = grp * 4 + j
                    top(lambda t, fc=fc, j=j, b=bank:
                            t.transpose(b[:, j, 0:128],
                                        src_b[:, fc * 128:(fc + 1) * 128],
                                        identb[:]),
                        list(rkeys) + ["identb"], [bname])
                vop(lambda v, grp=grp, n_in=n_in, b=bank:
                        v.tensor_copy(dst_fn(grp, n_in), b[:, 0:n_in, 0:128]),
                    [bname], dst_keys)

        K_XT, K_XN, K_EV, K_XNB, K_EVB, K_XNB2 = (
            slot_keys(s_) for s_ in ("xt", "xn", "ev", "xnb", "evb", "xnb2"))
        K_XQT, K_XQTB = bslot_keys("xqT"), bslot_keys("xqTb")
        K_EBA, K_EBB = bslot_keys("ebA"), bslot_keys("ebB")

        wqk_b = [wAb[:, 0:8192].rearrange("p (fc c) -> p fc c", c=1024),
                 wAb[:, 8192:16384].rearrange("p (fc c) -> p fc c", c=1024)]
        wv_b = [wDb[:, 0:4096].rearrange("p (fc c) -> p fc c", c=512),
                wDb[:, 4096:8192].rearrange("p (fc c) -> p fc c", c=512)]
        for fc in range(8):
            load(st("xt"), wqk_d[fc * 128:(fc + 1) * 128, :], K_XT, "LwA")
            sop(lambda s_, fc=fc: s_.activation(out=wqk_b[0][:, fc, :],
                                                in_=st("xt"), func=ACT.Identity),
                K_XT, ["wA"])
            vop(lambda v, fc=fc:
                    v.scalar_tensor_tensor(wqk_b[1][:, fc, :], st("xt"), 1.0,
                                           wqk_b[0][:, fc, :],
                                           op0=ALU.mult, op1=ALU.subtract),
                K_XT + ["wA"], ["wA"])
        for fc in range(8):
            load(st("xt", 0, 512), wv_d[fc * 128:(fc + 1) * 128, :], K_XT, "LwD")
            sop(lambda s_, fc=fc: s_.activation(out=wv_b[0][:, fc, :],
                                                in_=st("xt", 0, 512),
                                                func=ACT.Identity),
                K_XT, ["wD"])
            vop(lambda v, fc=fc:
                    v.scalar_tensor_tensor(wv_b[1][:, fc, :], st("xt", 0, 512), 1.0,
                                           wv_b[0][:, fc, :],
                                           op0=ALU.mult, op1=ALU.subtract),
                K_XT + ["wD"], ["wD"])
        xnT_b2 = [xnTb[:, 0:4096].rearrange("p (fc c) -> p fc c", c=512),
                  xnTb[:, 4096:8192].rearrange("p (fc c) -> p fc c", c=512)]


        def A_part1(s, t):
            def sub_steps(sub):
                g0 = s * SEQ + t * 512 + sub * 128
                if sub % 2 == 0:
                    xs, kxs = st("xt"), K_XT
                    xn_, kxn = st("xn"), K_XN
                    b6, k6a, k6b = bn6, ["bn6a"], ["bn6b"]
                    b2, k2 = bn2, ["bn2"]
                    sq, ksq = sqv, ["sqv"]
                    rst, krst = rstd, ["rstd"]
                    sb, KSB = qsb, ["qsb"]
                    ch = "Lxt"
                else:
                    xs = xn_ = qs4[:]
                    kxs = kxn = ["qs4"]
                    b6, k6a, k6b = sums[:, 128:140], ["sums"], ["sums"]
                    b2, k2 = sums[:, 140:142], ["sums"]
                    sq, ksq = sums[:, 142:143], ["sums"]
                    rst, krst = sums[:, 143:144], ["sums"]
                    sb, KSB = qb[:, 2048:4096], ["qb4", "qb5", "qb6", "qb7"]
                    ch = "Lxres"
                steps = [
                    lambda: load(xs, x_f[g0:g0 + 128, :], kxs, ch),
                    lambda: vop(lambda v: v.bn_stats(b6[:, 0:6], xs[:, 0:512]),
                                kxs, k6a),
                    lambda: vop(lambda v: v.bn_stats(b6[:, 6:12], xs[:, 512:1024]),
                                kxs, k6b),
                    lambda: vop(lambda v: v.bn_aggr(b2[:], b6[:]), k6a + k6b, k2),
                    lambda: sop(lambda s_: s_.activation(out=sq[:], in_=b2[:, 1:2],
                                                         func=ACT.Sqrt,
                                                         bias=cst[:, 0:1]),
                                k2 + ["cst"], ksq),
                    lambda: vop(lambda v: v.reciprocal(rst[:], sq[:]), ksq, krst),
                    lambda: vop(lambda v: v.tensor_scalar(xn_, xs, b2[:, 0:1],
                                                          rst[:],
                                                          op0=ALU.subtract,
                                                          op1=ALU.mult),
                                kxs + k2 + krst, kxn),
                    lambda: store(d_xn[g0:g0 + 128, :], xn_, kxn, "Oxn", ["d_xn"]),
                    lambda: sop(lambda s_: s_.activation(out=sb[:, 0:1024],
                                                         in_=xn_,
                                                         func=ACT.Identity),
                                kxn, KSB),
                    lambda: vop(lambda v: v.scalar_tensor_tensor(
                            sb[:, 1024:2048], xn_, 1.0, sb[:, 0:1024],
                            op0=ALU.mult, op1=ALU.subtract),
                        kxn + KSB, KSB),
                ]
                for hl in range(2):
                    steps.append(lambda hl=hl, sb=sb, KSB=KSB, sub=sub:
                                 transpose_tile_b(
                                     sb[:, hl * 1024:hl * 1024 + 1024],
                                     KSB, 8,
                                     lambda grp, n_in, sub=sub, hl=hl:
                                         xnT_b2[hl][:, grp * 4:grp * 4 + n_in,
                                                    sub * 128:(sub + 1) * 128],
                                     ["xnT"]))
                return steps

            for p0 in (0, 2):
                sA = sub_steps(p0)
                sB = sub_steps(p0 + 1)
                for j in range(max(len(sA), len(sB))):
                    if j < len(sA):
                        sA[j]()
                    if j < len(sB):
                        sB[j]()

        def A_part2(s, t):
            tok0 = t * 512
            evi = 0
            for hp in range(4):
                for par, (bias_t, bias_k, dst, dstk) in enumerate(
                        [(bq, "bq", d_q, "d_q"), (bk, "bk", d_k, "d_k")]):
                    pq_, pqn = (pA, "pA") if evi % 2 == 0 else (pB, "pB")
                    eb, kes = ((bst("ebA"), K_EBA) if evi % 2 == 0
                               else (bst("ebB"), K_EBB))
                    evi += 1
                    PASSES = ((0, 0), (1, 0), (0, 1))
                    for pi, (wh, xh) in enumerate(PASSES):
                        for fc in range(8):
                            wsl = wqk_b[wh][:, fc, :].rearrange(
                                "p (hp hd d par) -> p hp hd d par",
                                hp=4, hd=2, d=64, par=2)[:, hp, :, :, par]
                            top(lambda t_, wsl=wsl, fc=fc, pq_=pq_, xh=xh,
                                       first=(pi == 0 and fc == 0),
                                       last=(pi == 2 and fc == 7):
                                    t_.matmul(pq_[:], wsl, xnT_b2[xh][:, fc, :],
                                              start=first, stop=last),
                                ["wA", "xnT"], [pqn])
                    sop(lambda s_, bias_t=bias_t, hp=hp, pq_=pq_, eb=eb:
                            s_.activation(out=eb[:, 0:512], in_=pq_[:],
                                          func=ACT.Identity,
                                          bias=bias_t[:, hp:hp + 1]),
                        [pqn, bias_k], kes)
                    vop(lambda v, bias_t=bias_t, hp=hp, pq_=pq_, eb=eb:
                            v.scalar_tensor_tensor(eb[:, 512:1024], pq_[:],
                                                   bias_t[:, hp:hp + 1],
                                                   eb[:, 0:512],
                                                   op0=ALU.add, op1=ALU.subtract),
                        [pqn, bias_k] + kes, kes)
                    for hl in range(2):
                        store(dst[hl, s, hp, :, tok0:tok0 + 512],
                              eb[:, hl * 512:(hl + 1) * 512], kes,
                              "Oq" if par == 0 else "Ok", [dstk + str(s)])
            for sub in range(4):
                pq_, pqn = (pC, "pC") if sub % 2 == 0 else (pD, "pD")
                esl, kes = (("ev", K_EV) if sub % 2 == 0 else (("evb", K_EVB)))
                PASSES = ((0, 0), (1, 0), (0, 1))
                for pi, (xh, wh) in enumerate(PASSES):
                    for fc in range(8):
                        top(lambda t_, fc=fc, sub=sub, pq_=pq_, xh=xh, wh=wh,
                                   first=(pi == 0 and fc == 0),
                                   last=(pi == 2 and fc == 7):
                                t_.matmul(pq_[:],
                                          xnT_b2[xh][:, fc,
                                                     sub * 128:(sub + 1) * 128],
                                          wv_b[wh][:, fc, :],
                                          start=first, stop=last),
                            ["wD", "xnT"], [pqn])
                vop(lambda v, pq_=pq_, esl=esl:
                        v.scalar_tensor_tensor(st(esl, 0, 512), pq_[:], 1.0, bvt[:],
                                               op0=ALU.mult, op1=ALU.add),
                    [pqn, "bvt"], kes)
                sop(lambda s_, esl=esl:
                        s_.activation(out=vsb[:, 0:512], in_=st(esl, 0, 512),
                                      func=ACT.Identity),
                    kes, ["vsb"])
                vop(lambda v, esl=esl:
                        v.scalar_tensor_tensor(vsb[:, 512:1024], st(esl, 0, 512), 1.0,
                                               vsb[:, 0:512],
                                               op0=ALU.mult, op1=ALU.subtract),
                    kes + ["vsb"], ["vsb"])
                store(d_v[0, s, t * 512 + sub * 128:t * 512 + (sub + 1) * 128, :],
                      vsb[:, 0:512], ["vsb"], "Ov", ["d_v" + str(s)])
                store(d_v[1, s, t * 512 + sub * 128:t * 512 + (sub + 1) * 128, :],
                      vsb[:, 512:1024], ["vsb"], "Ov", ["d_v" + str(s)])

        SET_A2 = {"q1": (ob[:, 0:1024], ["ob0", "ob1"]),
                  "q2": (ob[:, 1024:2048], ["ob2", "ob3"]),
                  "q3": (st("bot"), slot_keys("bot")),
                  "sc": (sums[:, 0:64], ["sums"]),
                  "rs": (sums[:, 64:128], ["sums"]),
                  "sb": (qb[:, 0:2048], ["qb0", "qb1", "qb2", "qb3"])}

        def A_part3(s, t):
            def sub_steps(sub):
                g0 = s * SEQ + t * 512 + sub * 128
                xsl, kxs = (("xnb", K_XNB) if sub % 2 == 0 else ("xnb2", K_XNB2))
                qsl, kqs = (("xqT", K_XQT) if sub % 2 == 0 else ("xqTb", K_XQTB))
                SC = QSET0 if sub % 2 == 0 else SET_A2
                ch = "Lxnt" if sub % 2 == 0 else "Lgp"
                sb, KSB = SC["sb"]
                xqTs = bst(qsl)
                steps = [lambda: load(st(xsl), d_xn[g0:g0 + 128, :], kxs, ch,
                                      ["d_xn"])]
                steps += quantize_ops(st(xsl), kxs, D, SC, split=True)
                for hl in range(2):
                    def tr(hl=hl, xqTs=xqTs, sb=sb, KSB=KSB, kqs=kqs, g0=g0):
                        transpose_tile_b(sb[:, hl * 1024:hl * 1024 + 1024],
                                         KSB, 8,
                                         lambda grp, n_in, hl=hl, xqTs=xqTs:
                                             xqTs[:, hl * 1024 + grp * 4 * 128:
                                                  hl * 1024 + (grp * 4 + n_in) * 128],
                                         kqs)
                        store(d_xqT[hl, g0 // 128, :, :],
                              xqTs[:, hl * 1024:hl * 1024 + 1024],
                              kqs, "OxqT", ["d_xqT"])
                    steps.append(tr)
                return steps

            for p0 in (0, 2):
                sA = sub_steps(p0)
                sB = sub_steps(p0 + 1)
                for j in range(max(len(sA), len(sB))):
                    if j < len(sA):
                        sA[j]()
                    if j < len(sB):
                        sB[j]()

        tiles = [(s, t) for s in range(BPC) for t in range(4)]
        A_part1(*tiles[0])
        for i, (s, t) in enumerate(tiles):
            A_part2(s, t)
            if i + 1 < len(tiles):
                A_part1(*tiles[i + 1])
            A_part3(s, t)

        if dbg == "A":
            _os = [(ch, sch.count_of(ch)) for ch in
                   ("Oxn", "OxqT", "Oq", "Ok", "Ov") if sch.count_of(ch) > 0]
            sch.raw("gpsimd", lambda g, oc=tuple(_os):
                    [g.wait_ge(sems[c_], n_) for c_, n_ in oc] and None)
            _register(block, sch)
            return nc

        wo_b = [wDb[:, 0:4096].rearrange("p (hp c) -> p hp c", c=1024),
                wDb[:, 4096:8192].rearrange("p (hp c) -> p hp c", c=1024)]
        for hpw in range(4):
            load(st("xt"), wo_d[hpw * 128:(hpw + 1) * 128, :], K_XT, "LwD")
            sop(lambda s_, hpw=hpw: s_.activation(out=wo_b[0][:, hpw, :],
                                                  in_=st("xt"), func=ACT.Identity),
                K_XT, ["wD"])
            vop(lambda v, hpw=hpw:
                    v.scalar_tensor_tensor(wo_b[1][:, hpw, :], st("xt"), 1.0,
                                           wo_b[0][:, hpw, :],
                                           op0=ALU.mult, op1=ALU.subtract),
                K_XT + ["wD"], ["wD"])
        load(st("bot"), bo_d.partition_broadcast(128), slot_keys("bot"), "Lbias")
        bot = st("bot")
        K_BOT = slot_keys("bot")
        gmt, btt = ob[:, 0:1024], ob[:, 1024:2048]

        kvb = [wB[:, 0:8192].rearrange("p (hp t) -> p hp t", t=2048),
               wB[:, 8192:16384].rearrange("p (hp t) -> p hp t", t=2048)]
        qbb = [qb[:, 0:2048].rearrange("p (hp t) -> p hp t", t=512),
               qb[:, 2048:4096].rearrange("p (hp t) -> p hp t", t=512)]
        vvb = [wC[:, 0:8320].rearrange("p (kt h c) -> p kt h c", h=8, c=65),
               wC[:, 8320:16640].rearrange("p (kt h c) -> p kt h c", h=8, c=65)]
        K_E0, K_E1, K_Y = slot_keys("exp0"), slot_keys("exp1"), slot_keys("y")
        K_XNT, K_SO = slot_keys("xnt"), slot_keys("sout")
        K_E0B, K_E1B = slot_keys("exp0b"), slot_keys("exp1b")

        for s in range(BPC):
            for hl in range(2):
                load(kvb[hl], d_k[hl, s].rearrange("hp p t -> p hp t"),
                     ["wB"], "LwB", ["d_k" + str(s)])
            for hl in range(2):
                for kt in range(16):
                    load(vvb[hl][:, kt, :, 0:64],
                         d_v[hl, s, kt * 128:(kt + 1) * 128, :]
                         .rearrange("p (h c) -> p h c", c=64),
                         ["wC"], "LwC", ["d_v" + str(s)])
            gop(lambda g: g.memset(vvb[0][:, :, :, 64:65], 1.0), [], ["wC"])
            gop(lambda g: g.memset(vvb[1][:, :, :, 64:65], 0.0), [], ["wC"])
            for qt in range(4):
                for hl in range(2):
                    load(qbb[hl],
                         d_q[hl, s].rearrange("hp p t -> p hp t")
                         [:, :, qt * 512:(qt + 1) * 512],
                         ["qb%d" % (hl * 4 + i_) for i_ in range(4)], "Lqb",
                         ["d_q" + str(s)])
                PSLOTS = [(bst("ph%d" % i_), bslot_keys("ph%d" % i_))
                          for i_ in range(4)]
                for hp in range(4):
                    parts2 = [
                        [(pA, "pA", st("exp0"), K_E0, pC, "pC"),
                         (pB, "pB", st("exp1"), K_E1, pD, "pD")],
                        [(pE, "pE", st("exp0b"), K_E0B, pC, "pC"),
                         (pF, "pF", st("exp1b"), K_E1B, pD, "pD")],
                    ]
                    def qk_group(kc):
                        parts = parts2[kc % 2]
                        for hd, (ps_, psn, ex, exk, pv_, pvn) in enumerate(parts):
                            for pi, (xh, wh) in enumerate(((0, 0), (0, 1), (1, 0))):
                                top(lambda t_, hd=hd, hp=hp, kc=kc, ps_=ps_,
                                           xh=xh, wh=wh, first=(pi == 0),
                                           last=(pi == 2):
                                        t_.matmul(ps_[:],
                                                  kvb[wh][hd * 64:(hd + 1) * 64, hp,
                                                          kc * 128:(kc + 1) * 128],
                                                  qbb[xh][hd * 64:(hd + 1) * 64, hp, :],
                                                  start=first, stop=last,
                                                  tile_position=(hd * 64, 0)),
                                    ["wB", "qb%d" % hp, "qb%d" % (hp + 4)],
                                    [psn])

                    def exp_split(kc):
                        parts = parts2[kc % 2]
                        for hd, (ps_, psn, ex, exk, pv_, pvn) in enumerate(parts):
                            sop(lambda s_, ps_=ps_, ex=ex:
                                    s_.activation(out=ex, in_=ps_[:], func=ACT.Exp,
                                                  scale=ATTN_SCALE, bias=cst[:, 3:4]),
                                [psn], exk)
                            parena, PK = PSLOTS[(kc % 2) * 2 + hd]
                            ph = parena[:, 0:512]
                            pl = parena[:, 512:1024]
                            if kc % 4 < 2:
                                sop(lambda s_, ex=ex, ph=ph:
                                        s_.activation(out=ph, in_=ex,
                                                      func=ACT.Identity),
                                    exk, PK)
                            else:
                                vop(lambda v, ex=ex, ph=ph: v.tensor_copy(ph, ex),
                                    exk, PK)
                            vop(lambda v, ex=ex, ph=ph, pl=pl:
                                    v.scalar_tensor_tensor(pl, ex, 1.0, ph,
                                                           op0=ALU.mult,
                                                           op1=ALU.subtract),
                                exk + PK, PK)

                    def pv_group(kc):
                        parts = parts2[kc % 2]
                        for hd, (ps_, psn, ex, exk, pv_, pvn) in enumerate(parts):
                            h = 2 * hp + hd
                            parena, PK = PSLOTS[(kc % 2) * 2 + hd]
                            ph = parena[:, 0:512]
                            pl = parena[:, 512:1024]
                            for pi, (xh, wh) in enumerate(((0, 0), (0, 1), (1, 0))):
                                pp_ = ph if xh == 0 else pl
                                top(lambda t_, h=h, kc=kc, pp_=pp_, pv_=pv_, wh=wh,
                                           first=(pi == 0 and kc == 0),
                                           last=(pi == 2 and kc == 15):
                                        t_.matmul(pv_[0:65, :], vvb[wh][:, kc, h, :],
                                                  pp_, start=first, stop=last),
                                    ["wC"] + PK, [pvn])

                    qk_group(0)
                    qk_group(1)
                    for kc in range(16):
                        exp_split(kc)
                        if kc + 2 < 16:
                            qk_group(kc + 2)
                        pv_group(kc)
                    for hd, (_, _, _, _, pv_, pvn) in enumerate(parts2[0]):
                        vop(lambda v, pv_=pv_, hd=hd:
                                v.tensor_copy((sums if hd == 0 else qs2)[64:65, 0:512],
                                              pv_[64:65, :]),
                            [pvn], ["sums" if hd == 0 else "qs2"])
                        vop(lambda v, pv_=pv_, hd=hd, hp=hp:
                                v.tensor_copy(ob[hd * 64:(hd + 1) * 64,
                                                 hp * 512:(hp + 1) * 512],
                                              pv_[0:64, :]),
                            [pvn], ["ob%d" % hp])
                    dma("scalar", d_r[2 * hp:2 * hp + 1, :],
                        sums[64:65, 0:512], ["sums"], ["d_r"], "Or")
                    dma("scalar", d_r[2 * hp + 1:2 * hp + 2, :],
                        qs2[64:65, 0:512], ["qs2"], ["d_r"], "Or")
                    dma("sync", qs1[0:2, 0:512],
                        d_r[2 * hp:2 * hp + 2, :], ["d_r", "qs1"], ["qs1"], "Lrs")
                    vop(lambda v: v.reciprocal(qs1[0:2, 512:1024],
                                               qs1[0:2, 0:512]),
                        ["qs1"], ["qs1"]),
                    dma("scalar", d_r2[2 * hp:2 * hp + 2, :],
                        qs1[0:2, 512:1024], ["qs1"], ["d_r2"], "Or2")
                    rbf = qs3[:, 0:512]
                    for hd in range(2):
                        h = 2 * hp + hd
                        dma("sync", rbf[hd * 64:(hd + 1) * 64, :]
                            .rearrange("p (a q) -> p a q", a=1),
                            d_r2[h:h + 1, :].partition_broadcast(64),
                            ["d_r2"], ["qs3"], "Lrb")
                    vop(lambda v, hp=hp, rbf=rbf:
                            v.tensor_tensor(ob[:, hp * 512:(hp + 1) * 512],
                                            ob[:, hp * 512:(hp + 1) * 512],
                                            rbf[:], op=ALU.mult),
                        ["ob%d" % hp, "qs3"], ["ob%d" % hp])
                    obx = qb[:]
                    OBK = ["qb%d" % hp, "qb%d" % (hp + 4)]
                    sop(lambda s_, hp=hp, obx=obx:
                            s_.activation(out=obx[:, hp * 512:(hp + 1) * 512],
                                          in_=ob[:, hp * 512:(hp + 1) * 512],
                                          func=ACT.Identity),
                        ["ob%d" % hp], OBK)
                    vop(lambda v, hp=hp, obx=obx:
                            v.scalar_tensor_tensor(
                                obx[:, 2048 + hp * 512:2048 + (hp + 1) * 512],
                                ob[:, hp * 512:(hp + 1) * 512], 1.0,
                                obx[:, hp * 512:(hp + 1) * 512],
                                op0=ALU.mult, op1=ALU.subtract),
                        ["ob%d" % hp] + OBK, OBK)
                obx = qb[:]
                OBK = ["qb%d" % i_ for i_ in range(8)]
                obv = [obx[:, 0:2048].rearrange("p (hp t) -> p hp t", t=512),
                       obx[:, 2048:4096].rearrange("p (hp t) -> p hp t", t=512)]
                load(gmt, gam_d.partition_broadcast(128), ["ob0", "ob1"], "Lbias")
                load(btt, bet_d.partition_broadcast(128), ["ob2", "ob3"], "Lbias")
                for sub in range(4):
                    g0 = s * SEQ + qt * 512 + sub * 128
                    for fh, (py, pyn) in enumerate([(pA, "pA"), (pB, "pB")]):
                        for pi, (xh, wh) in enumerate(((0, 0), (0, 1), (1, 0))):
                            for hp in range(4):
                                top(lambda t_, hp=hp, sub=sub, fh=fh, py=py,
                                           xh=xh, wh=wh, obv=obv,
                                           first=(pi == 0 and hp == 0),
                                           last=(pi == 2 and hp == 3):
                                        t_.matmul(py[:],
                                                  obv[xh][:, hp,
                                                          sub * 128:(sub + 1) * 128],
                                                  wo_b[wh][:, hp,
                                                           fh * 512:(fh + 1) * 512],
                                                  start=first, stop=last),
                                    OBK + ["wD"], [pyn])
                        vop(lambda v, py=py, fh=fh:
                                v.scalar_tensor_tensor(
                                    st("y", fh * 512, (fh + 1) * 512), py[:], 1.0,
                                    bot[:, fh * 512:(fh + 1) * 512],
                                    op0=ALU.mult, op1=ALU.add),
                            [pyn] + K_BOT, K_Y)
                    vop(lambda v: v.bn_stats(bn6[:, 0:6], st("y", 0, 512)),
                        K_Y, ["bn6a"])
                    vop(lambda v: v.bn_stats(bn6[:, 6:12], st("y", 512, 1024)),
                        K_Y, ["bn6b"])
                    vop(lambda v: v.bn_aggr(bn2[:], bn6[:]), ["bn6a", "bn6b"], ["bn2"])
                    sop(lambda s_: s_.activation(out=sqv[:], in_=bn2[:, 1:2],
                                                 func=ACT.Sqrt, bias=cst[:, 0:1]),
                        ["bn2", "cst"], ["sqv"])
                    vop(lambda v: v.reciprocal(rstd[:], sqv[:]), ["sqv"], ["rstd"])
                    vop(lambda v: v.tensor_scalar(st("y"), st("y"), bn2[:, 0:1],
                                                  rstd[:], op0=ALU.subtract,
                                                  op1=ALU.mult),
                        K_Y + ["bn2", "rstd"], K_Y)
                    vop(lambda v: v.tensor_tensor(st("y"), st("y"), gmt, op=ALU.mult),
                        K_Y + ["ob0", "ob1"], K_Y)
                    vop(lambda v: v.tensor_tensor(st("y"), st("y"), btt, op=ALU.add),
                        K_Y + ["ob2", "ob3"], K_Y)
                    load(st("xnt"), d_xn[g0:g0 + 128, :], K_XNT, "Lxnt", ["d_xn"])
                    vop(lambda v: v.tensor_tensor(st("y"), st("y"), st("xnt"),
                                                  op=ALU.add),
                        K_Y + K_XNT, K_Y)
                    vop(lambda v: v.tensor_scalar(st("sout"), st("y"), RS2, None,
                                                  op0=ALU.mult),
                        K_Y, K_SO)
                    store(d_s[g0:g0 + 128, :], st("sout"), K_SO, "Os", ["d_s"])

        if dbg == "B":
            _os = [(ch, sch.count_of(ch)) for ch in
                   ("Oxn", "OxqT", "Oq", "Ok", "Ov", "Os", "Or", "Or2")
                   if sch.count_of(ch) > 0]
            sch.raw("gpsimd", lambda g, oc=tuple(_os):
                    [g.wait_ge(sems[c_], n_) for c_, n_ in oc] and None)
            _register(block, sch)
            return nc

        wgT = [wAb[:, 0:8192].rearrange("p (fc c) -> p fc c", c=1024),
               wAb[:, 8192:16384].rearrange("p (fc c) -> p fc c", c=1024)]
        wuT = [wB[:, 0:8192].rearrange("p (fc c) -> p fc c", c=1024),
               wB[:, 8192:16384].rearrange("p (fc c) -> p fc c", c=1024)]
        wdT = [wC[:, 0:8192].rearrange("p (nb c) -> p nb c", c=1024),
               wC[:, 8192:16384].rearrange("p (nb c) -> p nb c", c=1024)]
        def _v128(lo):
            return xnTb[:, lo:lo + 1024].rearrange("p (fc c) -> p fc c", c=128)
        xqT_t2 = [[_v128(0), _v128(1024)], [_v128(4096), _v128(5120)]]
        hqT2 = [[_v128(2048), _v128(3072)], [_v128(6144), _v128(7168)]]
        XQK = [["xnTa"], ["xnTb"]]
        HQK = [["xnTc"], ["xnTd"]]
        for _k in ("xnTa", "xnTb", "xnTc", "xnTd"):
            sch.op("gpsimd", lambda g: g.nop(), ["xnT"], [_k])
        K_H, K_SG, K_GP = slot_keys("h"), slot_keys("sg"), slot_keys("gp")
        K_XR, K_SR = slot_keys("xres"), slot_keys("sres")

        QSET1 = {"q1": (st("gp"), slot_keys("gp")),
                 "q2": (st("xres"), slot_keys("xres")),
                 "q3": (st("sres"), slot_keys("sres")),
                 "sc": (bvt[:, 0:64], ["bvt"]), "rs": (bvt[:, 64:128], ["bvt"]),
                 "sb": (qb[:, 0:2048], ["qb0", "qb1", "qb2", "qb3"])}

        for ci, (n0, nW) in enumerate(NSLICES):
            nbk = nW // 128

            def prep_steps(idx, SC, stg, kstg, ch, n0=n0, nW=nW, nbk=nbk):
                sb, KSB = SC["sb"]
                if idx < 2 * nbk:
                    wsrc, wdstT, wdk = ((wg_d, wgT, "wA") if idx < nbk
                                        else (wu_d, wuT, "wB"))
                    nb = idx % nbk
                    steps = [lambda: load(stg, wsrc[n0 + nb * 128:
                                                    n0 + (nb + 1) * 128, :],
                                          kstg, ch)]
                    steps += quantize_ops(stg, kstg, D, SC, split=True)
                    for hl in range(2):
                        steps.append(lambda hl=hl, nb=nb, wdstT=wdstT, wdk=wdk:
                                     transpose_tile_b(
                                         sb[:, hl * 1024:hl * 1024 + 1024],
                                         KSB, 8,
                                         lambda grp, n_in, nb=nb, w_=wdstT[hl]:
                                             w_[:, grp * 4:grp * 4 + n_in,
                                                nb * 128:(nb + 1) * 128],
                                         [wdk]))
                else:
                    fc = idx - 2 * nbk
                    steps = [lambda: load(stg[:, 0:nW],
                                          wd_d[fc * 128:(fc + 1) * 128,
                                               n0:n0 + nW], kstg, ch)]
                    steps += quantize_ops(stg[:, 0:nW], kstg, nW, SC, split=True)
                    for hl in range(2):
                        steps.append(lambda hl=hl, fc=fc:
                                     transpose_tile_b(
                                         sb[:, hl * 1024:hl * 1024 + nW],
                                         KSB, nbk,
                                         lambda grp, n_in, fc=fc, w_=wdT[hl]:
                                             w_[:, grp * 4:grp * 4 + n_in,
                                                fc * 128:(fc + 1) * 128],
                                         ["wC"]))
                return steps

            nunits = 2 * nbk + 8
            for ui in range(0, nunits, 2):
                sA = prep_steps(ui, QSET0, st("xt"), K_XT, "Lxt")
                sB = (prep_steps(ui + 1, QSET1, st("sg"), slot_keys("sg"), "Lxnt")
                      if ui + 1 < nunits else [])
                for j in range(max(len(sA), len(sB))):
                    if j < len(sA):
                        sA[j]()
                    if j < len(sB):
                        sB[j]()
            K_HB = slot_keys("hb")

            silu_jobs = {}

            def C_stage1(tt):
                g0 = tt * 128
                tb = tt % 2
                xqT_t = xqT_t2[tb]
                KXQ = XQK[tb]
                hsl, khs = (("h", K_H) if tb == 0 else ("hb", K_HB))
                silu_h = silu_jobs.setdefault(tt, [])
                for hl in range(2):
                    load(xqT_t[hl],
                         d_xqT[hl, tt, :, :].rearrange("p (fc c) -> p fc c", c=128),
                         KXQ, "LxnT", ["d_xqT"])
                nch = [(c * 512, min(512, nW - c * 512))
                       for c in range((nW + 511) // 512)]
                PASSES = ((0, 0), (0, 1), (1, 0))
                for ic, (c0, cw) in enumerate(nch):
                    pg, pgn = (pA, "pA") if (tt * 2 + ic) % 2 == 0 else (pC, "pC")
                    pu, pun = (pB, "pB") if (tt * 2 + ic) % 2 == 0 else (pD, "pD")
                    for pi, (xh, wh) in enumerate(PASSES):
                        for fc in range(8):
                            top(lambda t_, fc=fc, c0=c0, cw=cw, xh=xh, wh=wh, pg=pg,
                                       xqT_t=xqT_t,
                                       first=(pi == 0 and fc == 0),
                                       last=(pi == 2 and fc == 7):
                                    t_.matmul(pg[:, 0:cw], xqT_t[xh][:, fc, :],
                                              wgT[wh][:, fc, c0:c0 + cw],
                                              start=first, stop=last),
                                ["wA"] + KXQ, [pgn])
                    for pi, (xh, wh) in enumerate(PASSES):
                        for fc in range(8):
                            top(lambda t_, fc=fc, c0=c0, cw=cw, xh=xh, wh=wh, pu=pu,
                                       xqT_t=xqT_t,
                                       first=(pi == 0 and fc == 0),
                                       last=(pi == 2 and fc == 7):
                                    t_.matmul(pu[:, 0:cw], xqT_t[xh][:, fc, :],
                                              wuT[wh][:, fc, c0:c0 + cw],
                                              start=first, stop=last),
                                ["wB"] + KXQ, [pun])
                    silu_h.append((c0, cw, pg, pgn, pu, pun, hsl, khs))

            def C_stage1b(tt):
                for c0, cw, pg, pgn, pu, pun, hsl, khs in silu_jobs.pop(tt):
                    sop(lambda s_, cw=cw, pg=pg, c0=c0:
                            s_.activation(out=st("sg", c0, c0 + cw), in_=pg[:, 0:cw],
                                          func=ACT.Silu),
                        [pgn], K_SG)
                    vop(lambda v, c0=c0, cw=cw, pu=pu, hsl=hsl:
                            v.tensor_tensor(st(hsl, c0, c0 + cw), st("sg", c0, c0 + cw),
                                            pu[:, 0:cw], op=ALU.mult),
                        K_SG + [pun], khs)

            def C_stage2(tt):
                g0 = tt * 128
                tb = tt % 2
                hqT = hqT2[tb]
                KHQ = HQK[tb]
                hsl, khs = (("h", K_H) if tb == 0 else ("hb", K_HB))
                h0 = SLOT[hsl][0]
                if dbg == "full" and ci == 2:
                    store(d_h[g0:g0 + 128, :], stage[:, h0:h0 + nW], khs, "Oh", ["d_h"])
                quantize(stage[:, h0:h0 + nW], khs, nW, split=True, allv=True)
                for hl in range(2):
                    transpose_tile_b(qsb[:, hl * 1024:hl * 1024 + nW], ["qsb"], nbk,
                                     lambda grp, n_in, hl=hl, hqT=hqT:
                                         hqT[hl][:, grp * 4:grp * 4 + n_in, :],
                                     KHQ)
                PASSES = ((0, 0), (0, 1), (1, 0))
                for fh, (pd_, pdn) in enumerate([(pE, "pE"), (pF, "pF")]):
                    for pi, (xh, wh) in enumerate(PASSES):
                        for nb in range(nbk):
                            top(lambda t_, nb=nb, fh=fh, pd_=pd_, nbk=nbk, xh=xh, wh=wh,
                                       hqT=hqT,
                                       first=(pi == 0 and nb == 0),
                                       last=(pi == 2 and nb == nbk - 1):
                                    t_.matmul(pd_[:], hqT[xh][:, nb, :],
                                              wdT[wh][:, nb, fh * 512:(fh + 1) * 512],
                                              start=first, stop=last),
                                KHQ + ["wC"], [pdn])
                if ci == 0:
                    vop(lambda v: v.tensor_copy(st("gp", 0, 512), pE[:]),
                        ["pE"], K_GP)
                    vop(lambda v: v.tensor_copy(st("gp", 512, 1024), pF[:]),
                        ["pF"], K_GP)
                    store(d_g[g0:g0 + 128, :], st("gp", 0, 1024), K_GP, "Og", ["d_g"])
                else:
                    load(st("gp", 0, 1024), d_g[g0:g0 + 128, :], K_GP, "Lgp", ["d_g"])
                    vop(lambda v, hsl=hsl: v.scalar_tensor_tensor(
                            st(hsl, 0, 512), pE[:], 1.0, st("gp", 0, 512),
                            op0=ALU.mult, op1=ALU.add),
                        ["pE"] + K_GP, khs)
                    vop(lambda v, hsl=hsl: v.scalar_tensor_tensor(
                            st(hsl, 512, 1024), pF[:], 1.0, st("gp", 512, 1024),
                            op0=ALU.mult, op1=ALU.add),
                        ["pF"] + K_GP, khs)
                    if ci == 1:
                        store(d_g[g0:g0 + 128, :], st(hsl, 0, 1024), khs, "Og", ["d_g"])
                    else:
                        sop(lambda s_, hsl=hsl:
                                s_.activation(out=qs1[:], in_=st(hsl, 0, 1024),
                                              func=ACT.Silu),
                            khs, ["qs1"])
                        load(st("xres"), x_f[g0:g0 + 128, :], K_XR, "Lxres")
                        load(st("sres"), d_s[g0:g0 + 128, :], K_SR, "Lsres", ["d_s"])
                        vop(lambda v: v.tensor_tensor(qs2[:], qs1[:], st("sres"),
                                                      op=ALU.mult),
                            ["qs1"] + K_SR, ["qs2"])
                        vop(lambda v: v.tensor_tensor(qs3[:], qs2[:], st("xres"),
                                                      op=ALU.add),
                            ["qs2"] + K_XR, ["qs3"])
                        store(out_f[g0:g0 + 128, :], qs3[:], ["qs3"], "Oout")

            C_stage1(0)
            C_stage1b(0)
            for tt in range(32):
                if tt + 1 < 32:
                    C_stage1(tt + 1)
                C_stage2(tt)
                if tt + 1 < 32:
                    C_stage1b(tt + 1)

        _ostores = [(ch, sch.count_of(ch)) for ch in
                    ("Oxn", "OxqT", "Oq", "Ok", "Ov", "Os", "Og", "Oout",
                     "Or", "Or2", "Oh") if sch.count_of(ch) > 0]
        sch.raw("gpsimd", lambda g, oc=tuple(_ostores):
                [g.wait_ge(sems[c_], n_) for c_, n_ in oc] and None)
        _register(block, sch)
        return nc


def _register(block, sch):
    block.tensor(sch.replay("tensor"))
    block.vector(sch.replay("vector"))
    block.scalar(sch.replay("scalar"))
    block.gpsimd(sch.replay("gpsimd"))
    block.sync(sch.replay("sync"))


_WNAMES = [
    ("wqk", "L0_attn_W_qk"), ("bqk", "L0_attn_b_qk"), ("wv", "L0_attn_W_v"),
    ("bv", "L0_attn_b_v"), ("wo", "L0_attn_W_out"), ("bo", "L0_attn_b_out"),
    ("gam", "L0_attn_ln_gamma"), ("bet", "L0_attn_ln_beta"),
    ("wg", "L0_ffn_w_gate"), ("wu", "L0_ffn_w_up"), ("wd", "L0_ffn_w_down"),
]


def kernel(**inputs):
    x = np.ascontiguousarray(np.asarray(inputs["L0_attn_x"], dtype=np.float32))
    ws = {k: np.ascontiguousarray(np.asarray(inputs[n], dtype=np.float32))
          for k, n in _WNAMES}
    nc = build()
    in_maps = []
    for c in range(NCORES):
        m = {"x": x[c * BPC:(c + 1) * BPC]}
        m.update(ws)
        in_maps.append(m)
    res = run_bass_kernel_spmd(nc, in_maps, core_ids=list(range(NCORES)))
    out = np.concatenate([np.asarray(r["out"], dtype=np.float32)
                          for r in res.results], axis=0)
    return out


# version 35 (final)
# speedup vs baseline: 1.0175x; 1.0175x over previous
import numpy as np
from contextlib import ExitStack

import concourse.bass as bass
import concourse.mybir as mybir
from concourse.bass_utils import run_bass_kernel_spmd

F32 = mybir.dt.float32
BF16 = mybir.dt.bfloat16
ALU = mybir.AluOpType
ACT = mybir.ActivationFunctionType
AX = mybir.AxisListType

NCORES = 8
B, SEQ, D = 16, 2048, 1024
H, DH = 8, 64
VD = 512
FF = 2816
BPC = B // NCORES
TOK = BPC * SEQ
RS2 = float(2.0 ** -0.5)
LN_EPS = 1e-5
QTHRESH = [0.25, 0.75, 1.25, 1.75, 2.5, 3.5, 5.0]
EXPSHIFT = -20.0
ATTN_SCALE = 0.125
NSLICES = [(0, 1024), (1024, 896), (1920, 896)]

SLOT = {
    "xt": (0, 1024), "xn": (1024, 2048), "ev": (2048, 2560), "evb": (2560, 3072),
    "xnb": (3072, 4096), "xnb2": (4096, 5120),
    "exp0": (0, 512), "exp1": (512, 1024), "y": (1024, 2048),
    "xnt": (2048, 3072), "sout": (3072, 4096),
    "exp0b": (4096, 4608), "exp1b": (4608, 5120), "bot": (5120, 6144),
    "h": (0, 1024), "sg": (1024, 2048), "gp": (2048, 3072),
    "xres": (3072, 4096), "sres": (4096, 5120), "hb": (5120, 6144),
}
BSLOT = {
    "xqT": (0, 2048), "xqTb": (2048, 4096), "ebA": (4096, 5120),
    "ebB": (5120, 6144),
    "ph0": (0, 1024), "ph1": (1024, 2048), "ph2": (2048, 3072),
    "ph3": (3072, 4096),
}


def bslot_keys(name):
    lo, hi = BSLOT[name]
    return ["sb%d" % g for g in range(lo // 512, (hi + 511) // 512)]


def slot_keys(name):
    lo, hi = SLOT[name]
    g0, g1 = lo // 512, (hi + 511) // 512
    return ["stg%d" % g for g in range(g0, g1)]


class Sched:

    ENG_SEM = {"tensor": "T", "vector": "V", "scalar": "S", "gpsimd": "G", "sync": None}

    def __init__(self):
        self.items = []
        self.counts = {k: 0 for k in ("T", "V", "S", "G")}
        self.last_w = {}
        self.readers = {}
        self.marks = {}
        self.sems = None

    def count_of(self, sem):
        return self.counts.setdefault(sem, 0)

    def _wait_list(self, eng, reads, writes):
        need = {}
        for k in reads:
            w = self.last_w.get(k)
            if w:
                need[w[0]] = max(need.get(w[0], 0), w[1])
        for k in writes:
            w = self.last_w.get(k)
            if w:
                need[w[0]] = max(need.get(w[0], 0), w[1])
            for sem, cnt in self.readers.get(k, {}).items():
                need[sem] = max(need.get(sem, 0), cnt)
        out = []
        for sem, cnt in sorted(need.items()):
            if eng == "tensor" and sem == "T":
                continue
            mk = (eng, sem)
            if self.marks.get(mk, -1) >= cnt:
                continue
            out.append((sem, cnt))
        return out

    def op(self, eng, fn, reads=(), writes=(), dma=None):
        reads, writes = list(reads), list(writes)
        waits = self._wait_list(eng, reads, writes)
        sem = dma if dma else self.ENG_SEM[eng]
        amt = 16 if dma else 1
        if dma:
            prev = self.count_of(sem)
            if prev > 0 and self.marks.get((eng, sem), -1) < prev:
                waits = waits + [(sem, prev)]
        for sem_, cnt_ in waits:
            self.marks[(eng, sem_)] = cnt_
        self.counts[sem] = self.count_of(sem) + amt
        cnt_after = self.counts[sem]

        def closure(engine, _w=tuple(waits), _fn=fn, _sem=sem, _amt=amt):
            for s_, c_ in _w:
                engine.wait_ge(self.sems[s_], c_)
            instr = _fn(engine)
            if instr is not None:
                instr.then_inc(self.sems[_sem], _amt)

        self.items.append((eng, closure))
        for k in reads:
            self.readers.setdefault(k, {})[sem] = cnt_after
        for k in writes:
            self.last_w[k] = (sem, cnt_after)
            self.readers[k] = {}
        return cnt_after

    def raw(self, eng, fn):
        self.items.append((eng, lambda engine, _fn=fn: _fn(engine)))

    def replay(self, eng_name):
        def f(engine):
            for e_, closure in self.items:
                if e_ == eng_name:
                    closure(engine)
        return f


def build(dbg=None):
    nc = bass.Bass("TRN2", target_bir_lowering=False, detect_race_conditions=False)

    x_d = nc.dram_tensor("x", [BPC, SEQ, D], F32, kind="ExternalInput").ap()
    wqk_d = nc.dram_tensor("wqk", [D, D], F32, kind="ExternalInput").ap()
    bqk_d = nc.dram_tensor("bqk", [D], F32, kind="ExternalInput").ap()
    wv_d = nc.dram_tensor("wv", [D, VD], F32, kind="ExternalInput").ap()
    bv_d = nc.dram_tensor("bv", [VD], F32, kind="ExternalInput").ap()
    wo_d = nc.dram_tensor("wo", [VD, D], F32, kind="ExternalInput").ap()
    bo_d = nc.dram_tensor("bo", [D], F32, kind="ExternalInput").ap()
    gam_d = nc.dram_tensor("gam", [D], F32, kind="ExternalInput").ap()
    bet_d = nc.dram_tensor("bet", [D], F32, kind="ExternalInput").ap()
    wg_d = nc.dram_tensor("wg", [FF, D], F32, kind="ExternalInput").ap()
    wu_d = nc.dram_tensor("wu", [FF, D], F32, kind="ExternalInput").ap()
    wd_d = nc.dram_tensor("wd", [D, FF], F32, kind="ExternalInput").ap()
    out_d = nc.dram_tensor("out", [BPC, SEQ, D], F32, kind="ExternalOutput").ap()

    def scratch(name, shape, dt=F32):
        kind = "Internal" if dbg is None else "ExternalOutput"
        return nc.dram_tensor(name, shape, dt, kind=kind).ap()

    d_xn = scratch("d_xn", [TOK, D])
    d_xqT = scratch("d_xqT", [2, TOK // 128, 128, 8 * 128], BF16)
    d_q = scratch("d_q", [2, BPC, 4, 128, SEQ], BF16)
    d_k = scratch("d_k", [2, BPC, 4, 128, SEQ], BF16)
    d_v = scratch("d_v", [2, BPC, SEQ, VD], BF16)
    d_s = scratch("d_s", [TOK, D])
    d_g = scratch("d_g", [TOK, D])
    d_r = scratch("d_r", [8, 512])
    d_r2 = scratch("d_r2", [8, 512])
    d_h = scratch("d_h", [TOK, 896]) if dbg == "full" else None

    x_f = x_d.rearrange("s t d -> (s t) d")
    out_f = out_d.rearrange("s t d -> (s t) d")

    sch = Sched()

    with ExitStack() as ES:
        e = ES.enter_context
        wAb = e(nc.sbuf_tensor("wAb", [128, 16384], BF16))
        wB = e(nc.sbuf_tensor("wB", [128, 16384], BF16))
        wC = e(nc.sbuf_tensor("wC", [128, 16640], BF16))
        wDb = e(nc.sbuf_tensor("wDb", [128, 8192], BF16))
        xnTb = e(nc.sbuf_tensor("xnTb", [128, 8192], BF16))
        stage = e(nc.sbuf_tensor("stage", [128, 6144], F32))
        stgb = e(nc.sbuf_tensor("stgb", [128, 6144], BF16))
        qb = e(nc.sbuf_tensor("qb", [128, 4096], BF16))
        ob = e(nc.sbuf_tensor("ob", [128, 2048], F32))
        qs1 = e(nc.sbuf_tensor("qs1", [128, 1024], F32))
        qs2 = e(nc.sbuf_tensor("qs2", [128, 1024], F32))
        qs3 = e(nc.sbuf_tensor("qs3", [128, 1024], F32))
        qs4 = e(nc.sbuf_tensor("qs4", [128, 1024], F32))
        qsc = e(nc.sbuf_tensor("qsc", [128, 64], F32))
        qrs = e(nc.sbuf_tensor("qrs", [128, 64], F32))
        bn6 = e(nc.sbuf_tensor("bn6", [128, 12], F32))
        bn2 = e(nc.sbuf_tensor("bn2", [128, 2], F32))
        sqv = e(nc.sbuf_tensor("sqv", [128, 1], F32))
        rstd = e(nc.sbuf_tensor("rstd", [128, 1], F32))
        bq = e(nc.sbuf_tensor("bq", [128, 4], F32))
        bk = e(nc.sbuf_tensor("bk", [128, 4], F32))
        bvt = e(nc.sbuf_tensor("bvt", [128, 512], F32))
        cst = e(nc.sbuf_tensor("cst", [128, 4], F32))
        qsb = e(nc.sbuf_tensor("qsb", [128, 2048], BF16))
        vsb = e(nc.sbuf_tensor("vsb", [128, 1024], BF16))
        identb = e(nc.sbuf_tensor("identb", [128, 128], BF16))
        sums = e(nc.sbuf_tensor("sums", [128, 512], F32))
        pA = e(nc.psum_tensor("pA", [128, 512], F32))
        pB = e(nc.psum_tensor("pB", [128, 512], F32))
        pC = e(nc.psum_tensor("pC", [128, 512], F32))
        pD = e(nc.psum_tensor("pD", [128, 512], F32))
        pE = e(nc.psum_tensor("pE", [128, 512], F32))
        pF = e(nc.psum_tensor("pF", [128, 512], F32))
        pT1 = e(nc.psum_tensor("pT1", [128, 4, 256], BF16))
        pT2 = e(nc.psum_tensor("pT2", [128, 4, 256], BF16))

        _channels = ("T", "V", "S", "G",
                     "Lxt", "Lxnt", "Lgp", "Lxres", "Lsres", "LwA", "LwB",
                     "LwC", "LwD", "LxnT", "Lqb", "Lrs", "Lrb", "Lbias",
                     "Oxn", "OxqT", "Oq", "Ok", "Ov", "Os", "Og", "Oout",
                     "Or", "Or2", "Oh")
        sems = {s_: e(nc.semaphore("sem" + s_)) for s_ in _channels}
        sch.sems = sems
        block = e(nc.Block())

        def st(slot, lo=None, hi=None):
            a, b_ = SLOT[slot]
            if lo is not None:
                a, b_ = a + lo, a + hi
            return stage[:, a:b_]

        def dma(queue, out_ap, in_ap, reads, writes, ch, slow=False):
            sch.op(queue,
                   lambda g, o=out_ap, i=in_ap, s_=slow:
                       g.dma_start(out=o, in_=i, allow_slow_non_contiguous=s_),
                   reads=reads, writes=writes, dma=ch)

        def load(out_ap, in_ap, writes, ch, reads=()):
            dma("sync", out_ap, in_ap, reads, writes, ch)

        def store(out_ap, in_ap, reads, ch, writes=()):
            dma("gpsimd", out_ap, in_ap, reads, writes, ch)

        def vop(fn, reads, writes):
            sch.op("vector", fn, reads, writes)

        def sop(fn, reads, writes):
            sch.op("scalar", fn, reads, writes)

        def top(fn, reads, writes):
            sch.op("tensor", fn, reads, writes)

        def gop(fn, reads, writes):
            sch.op("gpsimd", fn, reads, writes)

        gop(lambda g: g.memset(identb[:], 0.0), [], ["identb"])
        gop(lambda g: g.affine_select(
                out=identb[:], in_=identb[:], compare_op=ALU.not_equal,
                fill=1.0, base=0, pattern=[[-1, 128]], channel_multiplier=1),
            ["identb"], ["identb"])

        for i_, val in enumerate([LN_EPS, -4.0, -6.0, EXPSHIFT]):
            gop(lambda g, i_=i_, val=val: g.memset(cst[:, i_:i_ + 1], val),
                [], ["cst"])
        bqk_r = bqk_d.rearrange("(hp hd d par) -> (hd d) hp par", hp=4, hd=2, d=64, par=2)
        dma("sync", bq[:], bqk_r[:, :, 0], [], ["bq"], "Lbias", slow=True)
        dma("sync", bk[:], bqk_r[:, :, 1], [], ["bk"], "Lbias", slow=True)
        load(bvt[:], bv_d.partition_broadcast(128), ["bvt"], "Lbias")

        def bst(slot):
            lo, hi = BSLOT[slot]
            return stgb[:, lo:hi]

        def quantize_ops(src_ap, rkeys, F, SC, split=False):
            q1, K1 = SC["q1"]
            q2, K2 = SC["q2"]
            q3, K3 = SC["q3"]
            scb, KSC = SC["sc"]
            rsb, KRS = SC["rs"]
            sb, KSB = SC["sb"]
            nb = F // 16
            src3 = src_ap.rearrange("p (nb bl) -> p nb bl", bl=16)
            ops = []
            E = ops.append
            E(lambda: vop(lambda v: v.tensor_reduce(scb[:, :nb], src3, axis=AX.X,
                                                    op=ALU.max,
                                                    apply_absolute_value=True),
                          rkeys, KSC))
            E(lambda: sop(lambda s: s.activation(out=q1[:, :F], in_=src_ap,
                                                 func=ACT.Abs), rkeys, K1))
            E(lambda: vop(lambda v: v.tensor_scalar(scb[:, :nb], scb[:, :nb],
                                                    1.0 / 6.0, 1e-12,
                                                    op0=ALU.mult, op1=ALU.max),
                          KSC, KSC))
            E(lambda: vop(lambda v: v.reciprocal(rsb[:, :nb], scb[:, :nb]),
                          KSC, KRS))
            rs_b = rsb[:, :nb].unsqueeze(2).broadcast_to([128, nb, 16])
            E(lambda: vop(lambda v: v.tensor_tensor(
                    q2[:, :F].rearrange("p (nb bl) -> p nb bl", bl=16),
                    q1[:, :F].rearrange("p (nb bl) -> p nb bl", bl=16),
                    rs_b, op=ALU.mult),
                K1 + KRS, K2))
            E(lambda: vop(lambda v: v.tensor_scalar(q3[:, :F], q2[:, :F],
                                                    QTHRESH[0], None,
                                                    op0=ALU.is_gt),
                          K2, K3))
            cnt = [(q3, K3), (q1, K1)]
            for j, c in enumerate(QTHRESH[1:]):
                a, ka = cnt[j % 2]
                o, ko = cnt[(j + 1) % 2]
                E(lambda c=c, a=a, o=o, ka=ka, ko=ko: vop(
                    lambda v: v.scalar_tensor_tensor(o[:, :F], q2[:, :F], c,
                                                     a[:, :F],
                                                     op0=ALU.is_gt, op1=ALU.add),
                    K2 + ka, ko))
            kk, kkk = cnt[0]
            E(lambda: sop(lambda s: s.activation(out=q1[:, :F], in_=kk[:, :F],
                                                 func=ACT.Relu, bias=cst[:, 1:2]),
                          kkk + ["cst"], K1))
            E(lambda: sop(lambda s: s.activation(out=q2[:, :F], in_=kk[:, :F],
                                                 func=ACT.Relu, bias=cst[:, 2:3]),
                          kkk + ["cst"], K2))
            E(lambda: vop(lambda v: v.scalar_tensor_tensor(q1[:, :F], q1[:, :F],
                                                           0.5, q2[:, :F],
                                                           op0=ALU.mult,
                                                           op1=ALU.add),
                          K1 + K2, K1))
            E(lambda: vop(lambda v: v.scalar_tensor_tensor(q1[:, :F], kk[:, :F],
                                                           0.5, q1[:, :F],
                                                           op0=ALU.mult,
                                                           op1=ALU.add),
                          kkk + K1, K1))
            E(lambda: sop(lambda s: s.activation(out=q2[:, :F], in_=src_ap,
                                                 func=ACT.Sign), rkeys, K2))
            E(lambda: vop(lambda v: v.tensor_tensor(q1[:, :F], q1[:, :F],
                                                    q2[:, :F], op=ALU.mult),
                          K1 + K2, K1))
            sc_b = scb[:, :nb].unsqueeze(2).broadcast_to([128, nb, 16])
            E(lambda: vop(lambda v: v.tensor_tensor(
                    src3, q1[:, :F].rearrange("p (nb bl) -> p nb bl", bl=16),
                    sc_b, op=ALU.mult),
                K1 + KSC, rkeys))
            if split:
                E(lambda: sop(lambda s: s.activation(out=sb[:, 0:F], in_=src_ap,
                                                     func=ACT.Identity),
                              rkeys, KSB))
                E(lambda: vop(lambda v: v.scalar_tensor_tensor(
                        sb[:, 1024:1024 + F], src_ap, 1.0, sb[:, 0:F],
                        op0=ALU.mult, op1=ALU.subtract),
                    rkeys + KSB, KSB))
            return ops

        QSET0 = {"q1": (qs1, ["qs1"]), "q2": (qs2, ["qs2"]),
                 "q3": (qs3, ["qs3"]),
                 "sc": (qsc, ["qsc"]), "rs": (qrs, ["qrs"]),
                 "sb": (qsb, ["qsb"])}

        def quantize(src_ap, rkeys, F, split=False):
            for f_ in quantize_ops(src_ap, rkeys, F, QSET0, split):
                f_()

        pT1b = pT1[:]
        pT2b = pT2[:]

        def transpose_tile_b(src_b, rkeys, nchunks, dst_fn, dst_keys):
            for grp in range((nchunks + 3) // 4):
                bank, bname = (pT1b, "pT1") if grp % 2 == 0 else (pT2b, "pT2")
                n_in = min(4, nchunks - grp * 4)
                for j in range(n_in):
                    fc = grp * 4 + j
                    top(lambda t, fc=fc, j=j, b=bank:
                            t.transpose(b[:, j, 0:128],
                                        src_b[:, fc * 128:(fc + 1) * 128],
                                        identb[:]),
                        list(rkeys) + ["identb"], [bname])
                vop(lambda v, grp=grp, n_in=n_in, b=bank:
                        v.tensor_copy(dst_fn(grp, n_in), b[:, 0:n_in, 0:128]),
                    [bname], dst_keys)

        K_XT, K_XN, K_EV, K_XNB, K_EVB, K_XNB2 = (
            slot_keys(s_) for s_ in ("xt", "xn", "ev", "xnb", "evb", "xnb2"))
        K_XQT, K_XQTB = bslot_keys("xqT"), bslot_keys("xqTb")
        K_EBA, K_EBB = bslot_keys("ebA"), bslot_keys("ebB")

        wqk_b = [wAb[:, 0:8192].rearrange("p (fc c) -> p fc c", c=1024),
                 wAb[:, 8192:16384].rearrange("p (fc c) -> p fc c", c=1024)]
        wv_b = [wDb[:, 0:4096].rearrange("p (fc c) -> p fc c", c=512),
                wDb[:, 4096:8192].rearrange("p (fc c) -> p fc c", c=512)]
        for fc in range(8):
            load(st("xt"), wqk_d[fc * 128:(fc + 1) * 128, :], K_XT, "LwA")
            sop(lambda s_, fc=fc: s_.activation(out=wqk_b[0][:, fc, :],
                                                in_=st("xt"), func=ACT.Identity),
                K_XT, ["wA"])
            vop(lambda v, fc=fc:
                    v.scalar_tensor_tensor(wqk_b[1][:, fc, :], st("xt"), 1.0,
                                           wqk_b[0][:, fc, :],
                                           op0=ALU.mult, op1=ALU.subtract),
                K_XT + ["wA"], ["wA"])
        for fc in range(8):
            load(st("xt", 0, 512), wv_d[fc * 128:(fc + 1) * 128, :], K_XT, "LwD")
            sop(lambda s_, fc=fc: s_.activation(out=wv_b[0][:, fc, :],
                                                in_=st("xt", 0, 512),
                                                func=ACT.Identity),
                K_XT, ["wD"])
            vop(lambda v, fc=fc:
                    v.scalar_tensor_tensor(wv_b[1][:, fc, :], st("xt", 0, 512), 1.0,
                                           wv_b[0][:, fc, :],
                                           op0=ALU.mult, op1=ALU.subtract),
                K_XT + ["wD"], ["wD"])
        xnT_b2 = [xnTb[:, 0:4096].rearrange("p (fc c) -> p fc c", c=512),
                  xnTb[:, 4096:8192].rearrange("p (fc c) -> p fc c", c=512)]


        def A_part1(s, t):
            def sub_steps(sub):
                g0 = s * SEQ + t * 512 + sub * 128
                if sub % 2 == 0:
                    xs, kxs = st("xt"), K_XT
                    xn_, kxn = st("xn"), K_XN
                    b6, k6a, k6b = bn6, ["bn6a"], ["bn6b"]
                    b2, k2 = bn2, ["bn2"]
                    sq, ksq = sqv, ["sqv"]
                    rst, krst = rstd, ["rstd"]
                    sb, KSB = qsb, ["qsb"]
                    ch = "Lxt"
                else:
                    xs = xn_ = qs4[:]
                    kxs = kxn = ["qs4"]
                    b6, k6a, k6b = sums[:, 128:140], ["sums"], ["sums"]
                    b2, k2 = sums[:, 140:142], ["sums"]
                    sq, ksq = sums[:, 142:143], ["sums"]
                    rst, krst = sums[:, 143:144], ["sums"]
                    sb, KSB = qb[:, 2048:4096], ["qb4", "qb5", "qb6", "qb7"]
                    ch = "Lxres"
                steps = [
                    lambda: load(xs, x_f[g0:g0 + 128, :], kxs, ch),
                    lambda: vop(lambda v: v.bn_stats(b6[:, 0:6], xs[:, 0:512]),
                                kxs, k6a),
                    lambda: vop(lambda v: v.bn_stats(b6[:, 6:12], xs[:, 512:1024]),
                                kxs, k6b),
                    lambda: vop(lambda v: v.bn_aggr(b2[:], b6[:]), k6a + k6b, k2),
                    lambda: sop(lambda s_: s_.activation(out=sq[:], in_=b2[:, 1:2],
                                                         func=ACT.Sqrt,
                                                         bias=cst[:, 0:1]),
                                k2 + ["cst"], ksq),
                    lambda: vop(lambda v: v.reciprocal(rst[:], sq[:]), ksq, krst),
                    lambda: vop(lambda v: v.tensor_scalar(xn_, xs, b2[:, 0:1],
                                                          rst[:],
                                                          op0=ALU.subtract,
                                                          op1=ALU.mult),
                                kxs + k2 + krst, kxn),
                    lambda: store(d_xn[g0:g0 + 128, :], xn_, kxn, "Oxn", ["d_xn"]),
                    lambda: sop(lambda s_: s_.activation(out=sb[:, 0:1024],
                                                         in_=xn_,
                                                         func=ACT.Identity),
                                kxn, KSB),
                    lambda: vop(lambda v: v.scalar_tensor_tensor(
                            sb[:, 1024:2048], xn_, 1.0, sb[:, 0:1024],
                            op0=ALU.mult, op1=ALU.subtract),
                        kxn + KSB, KSB),
                ]
                for hl in range(2):
                    steps.append(lambda hl=hl, sb=sb, KSB=KSB, sub=sub:
                                 transpose_tile_b(
                                     sb[:, hl * 1024:hl * 1024 + 1024],
                                     KSB, 8,
                                     lambda grp, n_in, sub=sub, hl=hl:
                                         xnT_b2[hl][:, grp * 4:grp * 4 + n_in,
                                                    sub * 128:(sub + 1) * 128],
                                     ["xnT"]))
                return steps

            for p0 in (0, 2):
                sA = sub_steps(p0)
                sB = sub_steps(p0 + 1)
                for j in range(max(len(sA), len(sB))):
                    if j < len(sA):
                        sA[j]()
                    if j < len(sB):
                        sB[j]()

        def A_part2(s, t):
            tok0 = t * 512
            evi = 0
            for hp in range(4):
                for par, (bias_t, bias_k, dst, dstk) in enumerate(
                        [(bq, "bq", d_q, "d_q"), (bk, "bk", d_k, "d_k")]):
                    pq_, pqn = (pA, "pA") if evi % 2 == 0 else (pB, "pB")
                    eb, kes = ((bst("ebA"), K_EBA) if evi % 2 == 0
                               else (bst("ebB"), K_EBB))
                    evi += 1
                    PASSES = ((0, 0), (1, 0), (0, 1))
                    for pi, (wh, xh) in enumerate(PASSES):
                        for fc in range(8):
                            wsl = wqk_b[wh][:, fc, :].rearrange(
                                "p (hp hd d par) -> p hp hd d par",
                                hp=4, hd=2, d=64, par=2)[:, hp, :, :, par]
                            top(lambda t_, wsl=wsl, fc=fc, pq_=pq_, xh=xh,
                                       first=(pi == 0 and fc == 0),
                                       last=(pi == 2 and fc == 7):
                                    t_.matmul(pq_[:], wsl, xnT_b2[xh][:, fc, :],
                                              start=first, stop=last),
                                ["wA", "xnT"], [pqn])
                    sop(lambda s_, bias_t=bias_t, hp=hp, pq_=pq_, eb=eb:
                            s_.activation(out=eb[:, 0:512], in_=pq_[:],
                                          func=ACT.Identity,
                                          bias=bias_t[:, hp:hp + 1]),
                        [pqn, bias_k], kes)
                    vop(lambda v, bias_t=bias_t, hp=hp, pq_=pq_, eb=eb:
                            v.scalar_tensor_tensor(eb[:, 512:1024], pq_[:],
                                                   bias_t[:, hp:hp + 1],
                                                   eb[:, 0:512],
                                                   op0=ALU.add, op1=ALU.subtract),
                        [pqn, bias_k] + kes, kes)
                    for hl in range(2):
                        store(dst[hl, s, hp, :, tok0:tok0 + 512],
                              eb[:, hl * 512:(hl + 1) * 512], kes,
                              "Oq" if par == 0 else "Ok", [dstk + str(s)])
            for sub in range(4):
                pq_, pqn = (pC, "pC") if sub % 2 == 0 else (pD, "pD")
                esl, kes = (("ev", K_EV) if sub % 2 == 0 else (("evb", K_EVB)))
                PASSES = ((0, 0), (1, 0), (0, 1))
                for pi, (xh, wh) in enumerate(PASSES):
                    for fc in range(8):
                        top(lambda t_, fc=fc, sub=sub, pq_=pq_, xh=xh, wh=wh,
                                   first=(pi == 0 and fc == 0),
                                   last=(pi == 2 and fc == 7):
                                t_.matmul(pq_[:],
                                          xnT_b2[xh][:, fc,
                                                     sub * 128:(sub + 1) * 128],
                                          wv_b[wh][:, fc, :],
                                          start=first, stop=last),
                            ["wD", "xnT"], [pqn])
                vop(lambda v, pq_=pq_, esl=esl:
                        v.scalar_tensor_tensor(st(esl, 0, 512), pq_[:], 1.0, bvt[:],
                                               op0=ALU.mult, op1=ALU.add),
                    [pqn, "bvt"], kes)
                sop(lambda s_, esl=esl:
                        s_.activation(out=vsb[:, 0:512], in_=st(esl, 0, 512),
                                      func=ACT.Identity),
                    kes, ["vsb"])
                vop(lambda v, esl=esl:
                        v.scalar_tensor_tensor(vsb[:, 512:1024], st(esl, 0, 512), 1.0,
                                               vsb[:, 0:512],
                                               op0=ALU.mult, op1=ALU.subtract),
                    kes + ["vsb"], ["vsb"])
                store(d_v[0, s, t * 512 + sub * 128:t * 512 + (sub + 1) * 128, :],
                      vsb[:, 0:512], ["vsb"], "Ov", ["d_v" + str(s)])
                store(d_v[1, s, t * 512 + sub * 128:t * 512 + (sub + 1) * 128, :],
                      vsb[:, 512:1024], ["vsb"], "Ov", ["d_v" + str(s)])

        SET_A2 = {"q1": (ob[:, 0:1024], ["ob0", "ob1"]),
                  "q2": (ob[:, 1024:2048], ["ob2", "ob3"]),
                  "q3": (st("bot"), slot_keys("bot")),
                  "sc": (sums[:, 0:64], ["sums"]),
                  "rs": (sums[:, 64:128], ["sums"]),
                  "sb": (qb[:, 0:2048], ["qb0", "qb1", "qb2", "qb3"])}

        def A_part3(s, t):
            def sub_steps(sub):
                g0 = s * SEQ + t * 512 + sub * 128
                xsl, kxs = (("xnb", K_XNB) if sub % 2 == 0 else ("xnb2", K_XNB2))
                qsl, kqs = (("xqT", K_XQT) if sub % 2 == 0 else ("xqTb", K_XQTB))
                SC = QSET0 if sub % 2 == 0 else SET_A2
                ch = "Lxnt" if sub % 2 == 0 else "Lgp"
                sb, KSB = SC["sb"]
                xqTs = bst(qsl)
                steps = [lambda: load(st(xsl), d_xn[g0:g0 + 128, :], kxs, ch,
                                      ["d_xn"])]
                steps += quantize_ops(st(xsl), kxs, D, SC, split=True)
                for hl in range(2):
                    def tr(hl=hl, xqTs=xqTs, sb=sb, KSB=KSB, kqs=kqs, g0=g0):
                        transpose_tile_b(sb[:, hl * 1024:hl * 1024 + 1024],
                                         KSB, 8,
                                         lambda grp, n_in, hl=hl, xqTs=xqTs:
                                             xqTs[:, hl * 1024 + grp * 4 * 128:
                                                  hl * 1024 + (grp * 4 + n_in) * 128],
                                         kqs)
                        store(d_xqT[hl, g0 // 128, :, :],
                              xqTs[:, hl * 1024:hl * 1024 + 1024],
                              kqs, "OxqT", ["d_xqT"])
                    steps.append(tr)
                return steps

            for p0 in (0, 2):
                sA = sub_steps(p0)
                sB = sub_steps(p0 + 1)
                for j in range(max(len(sA), len(sB))):
                    if j < len(sA):
                        sA[j]()
                    if j < len(sB):
                        sB[j]()

        tiles = [(s, t) for s in range(BPC) for t in range(4)]
        A_part1(*tiles[0])
        for i, (s, t) in enumerate(tiles):
            A_part2(s, t)
            if i + 1 < len(tiles):
                A_part1(*tiles[i + 1])
            A_part3(s, t)

        if dbg == "A":
            _os = [(ch, sch.count_of(ch)) for ch in
                   ("Oxn", "OxqT", "Oq", "Ok", "Ov") if sch.count_of(ch) > 0]
            sch.raw("gpsimd", lambda g, oc=tuple(_os):
                    [g.wait_ge(sems[c_], n_) for c_, n_ in oc] and None)
            _register(block, sch)
            return nc

        wo_b = [wDb[:, 0:4096].rearrange("p (hp c) -> p hp c", c=1024),
                wDb[:, 4096:8192].rearrange("p (hp c) -> p hp c", c=1024)]
        for hpw in range(4):
            load(st("xt"), wo_d[hpw * 128:(hpw + 1) * 128, :], K_XT, "LwD")
            sop(lambda s_, hpw=hpw: s_.activation(out=wo_b[0][:, hpw, :],
                                                  in_=st("xt"), func=ACT.Identity),
                K_XT, ["wD"])
            vop(lambda v, hpw=hpw:
                    v.scalar_tensor_tensor(wo_b[1][:, hpw, :], st("xt"), 1.0,
                                           wo_b[0][:, hpw, :],
                                           op0=ALU.mult, op1=ALU.subtract),
                K_XT + ["wD"], ["wD"])
        load(st("bot"), bo_d.partition_broadcast(128), slot_keys("bot"), "Lbias")
        bot = st("bot")
        K_BOT = slot_keys("bot")
        gmt, btt = ob[:, 0:1024], ob[:, 1024:2048]

        kvb = [wB[:, 0:8192].rearrange("p (hp t) -> p hp t", t=2048),
               wB[:, 8192:16384].rearrange("p (hp t) -> p hp t", t=2048)]
        qbb = [qb[:, 0:2048].rearrange("p (hp t) -> p hp t", t=512),
               qb[:, 2048:4096].rearrange("p (hp t) -> p hp t", t=512)]
        vvb = [wC[:, 0:8320].rearrange("p (kt h c) -> p kt h c", h=8, c=65),
               wC[:, 8320:16640].rearrange("p (kt h c) -> p kt h c", h=8, c=65)]
        K_E0, K_E1, K_Y = slot_keys("exp0"), slot_keys("exp1"), slot_keys("y")
        K_XNT, K_SO = slot_keys("xnt"), slot_keys("sout")
        K_E0B, K_E1B = slot_keys("exp0b"), slot_keys("exp1b")

        for s in range(BPC):
            for hl in range(2):
                load(kvb[hl], d_k[hl, s].rearrange("hp p t -> p hp t"),
                     ["wB"], "LwB", ["d_k" + str(s)])
            for hl in range(2):
                for kt in range(16):
                    load(vvb[hl][:, kt, :, 0:64],
                         d_v[hl, s, kt * 128:(kt + 1) * 128, :]
                         .rearrange("p (h c) -> p h c", c=64),
                         ["wC"], "LwC", ["d_v" + str(s)])
            gop(lambda g: g.memset(vvb[0][:, :, :, 64:65], 1.0), [], ["wC"])
            gop(lambda g: g.memset(vvb[1][:, :, :, 64:65], 0.0), [], ["wC"])
            for qt in range(4):
                for hl in range(2):
                    load(qbb[hl],
                         d_q[hl, s].rearrange("hp p t -> p hp t")
                         [:, :, qt * 512:(qt + 1) * 512],
                         ["qb%d" % (hl * 4 + i_) for i_ in range(4)], "Lqb",
                         ["d_q" + str(s)])
                PSLOTS = [(bst("ph%d" % i_), bslot_keys("ph%d" % i_))
                          for i_ in range(4)]
                for hp in range(4):
                    parts2 = [
                        [(pA, "pA", st("exp0"), K_E0, pC, "pC"),
                         (pB, "pB", st("exp1"), K_E1, pD, "pD")],
                        [(pE, "pE", st("exp0b"), K_E0B, pC, "pC"),
                         (pF, "pF", st("exp1b"), K_E1B, pD, "pD")],
                    ]
                    def qk_group(kc):
                        parts = parts2[kc % 2]
                        for hd, (ps_, psn, ex, exk, pv_, pvn) in enumerate(parts):
                            for pi, (xh, wh) in enumerate(((0, 0), (0, 1), (1, 0))):
                                top(lambda t_, hd=hd, hp=hp, kc=kc, ps_=ps_,
                                           xh=xh, wh=wh, first=(pi == 0),
                                           last=(pi == 2):
                                        t_.matmul(ps_[:],
                                                  kvb[wh][hd * 64:(hd + 1) * 64, hp,
                                                          kc * 128:(kc + 1) * 128],
                                                  qbb[xh][hd * 64:(hd + 1) * 64, hp, :],
                                                  start=first, stop=last,
                                                  tile_position=(hd * 64, 0)),
                                    ["wB", "qb%d" % hp, "qb%d" % (hp + 4)],
                                    [psn])

                    def exp_split(kc):
                        parts = parts2[kc % 2]
                        for hd, (ps_, psn, ex, exk, pv_, pvn) in enumerate(parts):
                            sop(lambda s_, ps_=ps_, ex=ex:
                                    s_.activation(out=ex, in_=ps_[:], func=ACT.Exp,
                                                  scale=ATTN_SCALE, bias=cst[:, 3:4]),
                                [psn], exk)
                            parena, PK = PSLOTS[(kc % 2) * 2 + hd]
                            ph = parena[:, 0:512]
                            pl = parena[:, 512:1024]
                            if kc % 4 < 2:
                                sop(lambda s_, ex=ex, ph=ph:
                                        s_.activation(out=ph, in_=ex,
                                                      func=ACT.Identity),
                                    exk, PK)
                            else:
                                vop(lambda v, ex=ex, ph=ph: v.tensor_copy(ph, ex),
                                    exk, PK)
                            vop(lambda v, ex=ex, ph=ph, pl=pl:
                                    v.scalar_tensor_tensor(pl, ex, 1.0, ph,
                                                           op0=ALU.mult,
                                                           op1=ALU.subtract),
                                exk + PK, PK)

                    def pv_group(kc):
                        parts = parts2[kc % 2]
                        for hd, (ps_, psn, ex, exk, pv_, pvn) in enumerate(parts):
                            h = 2 * hp + hd
                            parena, PK = PSLOTS[(kc % 2) * 2 + hd]
                            ph = parena[:, 0:512]
                            pl = parena[:, 512:1024]
                            for pi, (xh, wh) in enumerate(((0, 0), (0, 1), (1, 0))):
                                pp_ = ph if xh == 0 else pl
                                top(lambda t_, h=h, kc=kc, pp_=pp_, pv_=pv_, wh=wh,
                                           first=(pi == 0 and kc == 0),
                                           last=(pi == 2 and kc == 15):
                                        t_.matmul(pv_[0:65, :], vvb[wh][:, kc, h, :],
                                                  pp_, start=first, stop=last),
                                    ["wC"] + PK, [pvn])

                    qk_group(0)
                    qk_group(1)
                    for kc in range(16):
                        exp_split(kc)
                        if kc + 2 < 16:
                            qk_group(kc + 2)
                        pv_group(kc)
                    for hd, (_, _, _, _, pv_, pvn) in enumerate(parts2[0]):
                        vop(lambda v, pv_=pv_, hd=hd:
                                v.tensor_copy((sums if hd == 0 else qs2)[64:65, 0:512],
                                              pv_[64:65, :]),
                            [pvn], ["sums" if hd == 0 else "qs2"])
                        vop(lambda v, pv_=pv_, hd=hd, hp=hp:
                                v.tensor_copy(ob[hd * 64:(hd + 1) * 64,
                                                 hp * 512:(hp + 1) * 512],
                                              pv_[0:64, :]),
                            [pvn], ["ob%d" % hp])
                    dma("scalar", d_r[2 * hp:2 * hp + 1, :],
                        sums[64:65, 0:512], ["sums"], ["d_r"], "Or")
                    dma("scalar", d_r[2 * hp + 1:2 * hp + 2, :],
                        qs2[64:65, 0:512], ["qs2"], ["d_r"], "Or")
                    dma("sync", qs1[0:2, 0:512],
                        d_r[2 * hp:2 * hp + 2, :], ["d_r", "qs1"], ["qs1"], "Lrs")
                    vop(lambda v: v.reciprocal(qs1[0:2, 512:1024],
                                               qs1[0:2, 0:512]),
                        ["qs1"], ["qs1"]),
                    dma("scalar", d_r2[2 * hp:2 * hp + 2, :],
                        qs1[0:2, 512:1024], ["qs1"], ["d_r2"], "Or2")
                    rbf = qs3[:, 0:512]
                    for hd in range(2):
                        h = 2 * hp + hd
                        dma("sync", rbf[hd * 64:(hd + 1) * 64, :]
                            .rearrange("p (a q) -> p a q", a=1),
                            d_r2[h:h + 1, :].partition_broadcast(64),
                            ["d_r2"], ["qs3"], "Lrb")
                    vop(lambda v, hp=hp, rbf=rbf:
                            v.tensor_tensor(ob[:, hp * 512:(hp + 1) * 512],
                                            ob[:, hp * 512:(hp + 1) * 512],
                                            rbf[:], op=ALU.mult),
                        ["ob%d" % hp, "qs3"], ["ob%d" % hp])
                    obx = qb[:]
                    OBK = ["qb%d" % hp, "qb%d" % (hp + 4)]
                    sop(lambda s_, hp=hp, obx=obx:
                            s_.activation(out=obx[:, hp * 512:(hp + 1) * 512],
                                          in_=ob[:, hp * 512:(hp + 1) * 512],
                                          func=ACT.Identity),
                        ["ob%d" % hp], OBK)
                    vop(lambda v, hp=hp, obx=obx:
                            v.scalar_tensor_tensor(
                                obx[:, 2048 + hp * 512:2048 + (hp + 1) * 512],
                                ob[:, hp * 512:(hp + 1) * 512], 1.0,
                                obx[:, hp * 512:(hp + 1) * 512],
                                op0=ALU.mult, op1=ALU.subtract),
                        ["ob%d" % hp] + OBK, OBK)
                obx = qb[:]
                OBK = ["qb%d" % i_ for i_ in range(8)]
                obv = [obx[:, 0:2048].rearrange("p (hp t) -> p hp t", t=512),
                       obx[:, 2048:4096].rearrange("p (hp t) -> p hp t", t=512)]
                load(gmt, gam_d.partition_broadcast(128), ["ob0", "ob1"], "Lbias")
                load(btt, bet_d.partition_broadcast(128), ["ob2", "ob3"], "Lbias")
                for sub in range(4):
                    g0 = s * SEQ + qt * 512 + sub * 128
                    for fh, (py, pyn) in enumerate([(pA, "pA"), (pB, "pB")]):
                        for pi, (xh, wh) in enumerate(((0, 0), (0, 1), (1, 0))):
                            for hp in range(4):
                                top(lambda t_, hp=hp, sub=sub, fh=fh, py=py,
                                           xh=xh, wh=wh, obv=obv,
                                           first=(pi == 0 and hp == 0),
                                           last=(pi == 2 and hp == 3):
                                        t_.matmul(py[:],
                                                  obv[xh][:, hp,
                                                          sub * 128:(sub + 1) * 128],
                                                  wo_b[wh][:, hp,
                                                           fh * 512:(fh + 1) * 512],
                                                  start=first, stop=last),
                                    OBK + ["wD"], [pyn])
                        vop(lambda v, py=py, fh=fh:
                                v.scalar_tensor_tensor(
                                    st("y", fh * 512, (fh + 1) * 512), py[:], 1.0,
                                    bot[:, fh * 512:(fh + 1) * 512],
                                    op0=ALU.mult, op1=ALU.add),
                            [pyn] + K_BOT, K_Y)
                    vop(lambda v: v.bn_stats(bn6[:, 0:6], st("y", 0, 512)),
                        K_Y, ["bn6a"])
                    vop(lambda v: v.bn_stats(bn6[:, 6:12], st("y", 512, 1024)),
                        K_Y, ["bn6b"])
                    vop(lambda v: v.bn_aggr(bn2[:], bn6[:]), ["bn6a", "bn6b"], ["bn2"])
                    sop(lambda s_: s_.activation(out=sqv[:], in_=bn2[:, 1:2],
                                                 func=ACT.Sqrt, bias=cst[:, 0:1]),
                        ["bn2", "cst"], ["sqv"])
                    vop(lambda v: v.reciprocal(rstd[:], sqv[:]), ["sqv"], ["rstd"])
                    vop(lambda v: v.tensor_scalar(st("y"), st("y"), bn2[:, 0:1],
                                                  rstd[:], op0=ALU.subtract,
                                                  op1=ALU.mult),
                        K_Y + ["bn2", "rstd"], K_Y)
                    vop(lambda v: v.tensor_tensor(st("y"), st("y"), gmt, op=ALU.mult),
                        K_Y + ["ob0", "ob1"], K_Y)
                    vop(lambda v: v.tensor_tensor(st("y"), st("y"), btt, op=ALU.add),
                        K_Y + ["ob2", "ob3"], K_Y)
                    load(st("xnt"), d_xn[g0:g0 + 128, :], K_XNT, "Lxnt", ["d_xn"])
                    vop(lambda v: v.tensor_tensor(st("y"), st("y"), st("xnt"),
                                                  op=ALU.add),
                        K_Y + K_XNT, K_Y)
                    vop(lambda v: v.tensor_scalar(st("sout"), st("y"), RS2, None,
                                                  op0=ALU.mult),
                        K_Y, K_SO)
                    store(d_s[g0:g0 + 128, :], st("sout"), K_SO, "Os", ["d_s"])

        if dbg == "B":
            _os = [(ch, sch.count_of(ch)) for ch in
                   ("Oxn", "OxqT", "Oq", "Ok", "Ov", "Os", "Or", "Or2")
                   if sch.count_of(ch) > 0]
            sch.raw("gpsimd", lambda g, oc=tuple(_os):
                    [g.wait_ge(sems[c_], n_) for c_, n_ in oc] and None)
            _register(block, sch)
            return nc

        wgT = [wAb[:, 0:8192].rearrange("p (fc c) -> p fc c", c=1024),
               wAb[:, 8192:16384].rearrange("p (fc c) -> p fc c", c=1024)]
        wuT = [wB[:, 0:8192].rearrange("p (fc c) -> p fc c", c=1024),
               wB[:, 8192:16384].rearrange("p (fc c) -> p fc c", c=1024)]
        wdT = [wC[:, 0:8192].rearrange("p (nb c) -> p nb c", c=1024),
               wC[:, 8192:16384].rearrange("p (nb c) -> p nb c", c=1024)]
        def _v128(lo):
            return xnTb[:, lo:lo + 1024].rearrange("p (fc c) -> p fc c", c=128)
        xqT_t2 = [[_v128(0), _v128(1024)], [_v128(4096), _v128(5120)]]
        hqT2 = [[_v128(2048), _v128(3072)], [_v128(6144), _v128(7168)]]
        XQK = [["xnTa"], ["xnTb"]]
        HQK = [["xnTc"], ["xnTd"]]
        for _k in ("xnTa", "xnTb", "xnTc", "xnTd"):
            sch.op("gpsimd", lambda g: g.nop(), ["xnT"], [_k])
        K_H, K_SG, K_GP = slot_keys("h"), slot_keys("sg"), slot_keys("gp")
        K_XR, K_SR = slot_keys("xres"), slot_keys("sres")

        QSET1 = {"q1": (st("gp"), slot_keys("gp")),
                 "q2": (st("xres"), slot_keys("xres")),
                 "q3": (st("sres"), slot_keys("sres")),
                 "sc": (bvt[:, 0:64], ["bvt"]), "rs": (bvt[:, 64:128], ["bvt"]),
                 "sb": (qb[:, 0:2048], ["qb0", "qb1", "qb2", "qb3"])}

        for ci, (n0, nW) in enumerate(NSLICES):
            nbk = nW // 128

            def prep_steps(idx, SC, stg, kstg, ch, n0=n0, nW=nW, nbk=nbk):
                sb, KSB = SC["sb"]
                if idx < 2 * nbk:
                    wsrc, wdstT, wdk = ((wg_d, wgT, "wA") if idx < nbk
                                        else (wu_d, wuT, "wB"))
                    nb = idx % nbk
                    steps = [lambda: load(stg, wsrc[n0 + nb * 128:
                                                    n0 + (nb + 1) * 128, :],
                                          kstg, ch)]
                    steps += quantize_ops(stg, kstg, D, SC, split=True)
                    for hl in range(2):
                        steps.append(lambda hl=hl, nb=nb, wdstT=wdstT, wdk=wdk:
                                     transpose_tile_b(
                                         sb[:, hl * 1024:hl * 1024 + 1024],
                                         KSB, 8,
                                         lambda grp, n_in, nb=nb, w_=wdstT[hl]:
                                             w_[:, grp * 4:grp * 4 + n_in,
                                                nb * 128:(nb + 1) * 128],
                                         [wdk]))
                else:
                    fc = idx - 2 * nbk
                    steps = [lambda: load(stg[:, 0:nW],
                                          wd_d[fc * 128:(fc + 1) * 128,
                                               n0:n0 + nW], kstg, ch)]
                    steps += quantize_ops(stg[:, 0:nW], kstg, nW, SC, split=True)
                    for hl in range(2):
                        steps.append(lambda hl=hl, fc=fc:
                                     transpose_tile_b(
                                         sb[:, hl * 1024:hl * 1024 + nW],
                                         KSB, nbk,
                                         lambda grp, n_in, fc=fc, w_=wdT[hl]:
                                             w_[:, grp * 4:grp * 4 + n_in,
                                                fc * 128:(fc + 1) * 128],
                                         ["wC"]))
                return steps

            nunits = 2 * nbk + 8
            for ui in range(0, nunits, 2):
                sA = prep_steps(ui, QSET0, st("xt"), K_XT, "Lxt")
                sB = (prep_steps(ui + 1, QSET1, st("sg"), slot_keys("sg"), "Lxnt")
                      if ui + 1 < nunits else [])
                for j in range(max(len(sA), len(sB))):
                    if j < len(sA):
                        sA[j]()
                    if j < len(sB):
                        sB[j]()
            K_HB = slot_keys("hb")

            silu_jobs = {}

            def C_stage1(tt):
                g0 = tt * 128
                tb = tt % 2
                xqT_t = xqT_t2[tb]
                KXQ = XQK[tb]
                hsl, khs = (("h", K_H) if tb == 0 else ("hb", K_HB))
                silu_h = silu_jobs.setdefault(tt, [])
                for hl in range(2):
                    load(xqT_t[hl],
                         d_xqT[hl, tt, :, :].rearrange("p (fc c) -> p fc c", c=128),
                         KXQ, "LxnT", ["d_xqT"])
                nch = [(c * 512, min(512, nW - c * 512))
                       for c in range((nW + 511) // 512)]
                PASSES = ((0, 0), (0, 1), (1, 0))
                for ic, (c0, cw) in enumerate(nch):
                    pg, pgn = (pA, "pA") if (tt * 2 + ic) % 2 == 0 else (pC, "pC")
                    pu, pun = (pB, "pB") if (tt * 2 + ic) % 2 == 0 else (pD, "pD")
                    for pi, (xh, wh) in enumerate(PASSES):
                        for fc in range(8):
                            top(lambda t_, fc=fc, c0=c0, cw=cw, xh=xh, wh=wh, pg=pg,
                                       xqT_t=xqT_t,
                                       first=(pi == 0 and fc == 0),
                                       last=(pi == 2 and fc == 7):
                                    t_.matmul(pg[:, 0:cw], xqT_t[xh][:, fc, :],
                                              wgT[wh][:, fc, c0:c0 + cw],
                                              start=first, stop=last),
                                ["wA"] + KXQ, [pgn])
                    for pi, (xh, wh) in enumerate(PASSES):
                        for fc in range(8):
                            top(lambda t_, fc=fc, c0=c0, cw=cw, xh=xh, wh=wh, pu=pu,
                                       xqT_t=xqT_t,
                                       first=(pi == 0 and fc == 0),
                                       last=(pi == 2 and fc == 7):
                                    t_.matmul(pu[:, 0:cw], xqT_t[xh][:, fc, :],
                                              wuT[wh][:, fc, c0:c0 + cw],
                                              start=first, stop=last),
                                ["wB"] + KXQ, [pun])
                    silu_h.append((c0, cw, pg, pgn, pu, pun, hsl, khs))

            def C_stage1b(tt):
                for c0, cw, pg, pgn, pu, pun, hsl, khs in silu_jobs.pop(tt):
                    sop(lambda s_, cw=cw, pg=pg, c0=c0:
                            s_.activation(out=st("sg", c0, c0 + cw), in_=pg[:, 0:cw],
                                          func=ACT.Silu),
                        [pgn], K_SG)
                    vop(lambda v, c0=c0, cw=cw, pu=pu, hsl=hsl:
                            v.tensor_tensor(st(hsl, c0, c0 + cw), st("sg", c0, c0 + cw),
                                            pu[:, 0:cw], op=ALU.mult),
                        K_SG + [pun], khs)

            def C_stage2(tt):
                g0 = tt * 128
                tb = tt % 2
                hqT = hqT2[tb]
                KHQ = HQK[tb]
                hsl, khs = (("h", K_H) if tb == 0 else ("hb", K_HB))
                h0 = SLOT[hsl][0]
                if dbg == "full" and ci == 2:
                    store(d_h[g0:g0 + 128, :], stage[:, h0:h0 + nW], khs, "Oh", ["d_h"])
                quantize(stage[:, h0:h0 + nW], khs, nW, split=True)
                for hl in range(2):
                    transpose_tile_b(qsb[:, hl * 1024:hl * 1024 + nW], ["qsb"], nbk,
                                     lambda grp, n_in, hl=hl, hqT=hqT:
                                         hqT[hl][:, grp * 4:grp * 4 + n_in, :],
                                     KHQ)
                PASSES = ((0, 0), (0, 1), (1, 0))
                for fh, (pd_, pdn) in enumerate([(pE, "pE"), (pF, "pF")]):
                    for pi, (xh, wh) in enumerate(PASSES):
                        for nb in range(nbk):
                            top(lambda t_, nb=nb, fh=fh, pd_=pd_, nbk=nbk, xh=xh, wh=wh,
                                       hqT=hqT,
                                       first=(pi == 0 and nb == 0),
                                       last=(pi == 2 and nb == nbk - 1):
                                    t_.matmul(pd_[:], hqT[xh][:, nb, :],
                                              wdT[wh][:, nb, fh * 512:(fh + 1) * 512],
                                              start=first, stop=last),
                                KHQ + ["wC"], [pdn])
                if ci == 0:
                    vop(lambda v: v.tensor_copy(st("gp", 0, 512), pE[:]),
                        ["pE"], K_GP)
                    vop(lambda v: v.tensor_copy(st("gp", 512, 1024), pF[:]),
                        ["pF"], K_GP)
                    store(d_g[g0:g0 + 128, :], st("gp", 0, 1024), K_GP, "Og", ["d_g"])
                else:
                    load(st("gp", 0, 1024), d_g[g0:g0 + 128, :], K_GP, "Lgp", ["d_g"])
                    vop(lambda v, hsl=hsl: v.scalar_tensor_tensor(
                            st(hsl, 0, 512), pE[:], 1.0, st("gp", 0, 512),
                            op0=ALU.mult, op1=ALU.add),
                        ["pE"] + K_GP, khs)
                    vop(lambda v, hsl=hsl: v.scalar_tensor_tensor(
                            st(hsl, 512, 1024), pF[:], 1.0, st("gp", 512, 1024),
                            op0=ALU.mult, op1=ALU.add),
                        ["pF"] + K_GP, khs)
                    if ci == 1:
                        store(d_g[g0:g0 + 128, :], st(hsl, 0, 1024), khs, "Og", ["d_g"])
                    else:
                        sop(lambda s_, hsl=hsl:
                                s_.activation(out=qs1[:], in_=st(hsl, 0, 1024),
                                              func=ACT.Silu),
                            khs, ["qs1"])
                        load(st("xres"), x_f[g0:g0 + 128, :], K_XR, "Lxres")
                        load(st("sres"), d_s[g0:g0 + 128, :], K_SR, "Lsres", ["d_s"])
                        vop(lambda v: v.tensor_tensor(qs2[:], qs1[:], st("sres"),
                                                      op=ALU.mult),
                            ["qs1"] + K_SR, ["qs2"])
                        vop(lambda v: v.tensor_tensor(qs3[:], qs2[:], st("xres"),
                                                      op=ALU.add),
                            ["qs2"] + K_XR, ["qs3"])
                        store(out_f[g0:g0 + 128, :], qs3[:], ["qs3"], "Oout")

            C_stage1(0)
            C_stage1b(0)
            for tt in range(32):
                if tt + 1 < 32:
                    C_stage1(tt + 1)
                C_stage2(tt)
                if tt + 1 < 32:
                    C_stage1b(tt + 1)

        _ostores = [(ch, sch.count_of(ch)) for ch in
                    ("Oxn", "OxqT", "Oq", "Ok", "Ov", "Os", "Og", "Oout",
                     "Or", "Or2", "Oh") if sch.count_of(ch) > 0]
        sch.raw("gpsimd", lambda g, oc=tuple(_ostores):
                [g.wait_ge(sems[c_], n_) for c_, n_ in oc] and None)
        _register(block, sch)
        return nc


def _register(block, sch):
    block.tensor(sch.replay("tensor"))
    block.vector(sch.replay("vector"))
    block.scalar(sch.replay("scalar"))
    block.gpsimd(sch.replay("gpsimd"))
    block.sync(sch.replay("sync"))


_WNAMES = [
    ("wqk", "L0_attn_W_qk"), ("bqk", "L0_attn_b_qk"), ("wv", "L0_attn_W_v"),
    ("bv", "L0_attn_b_v"), ("wo", "L0_attn_W_out"), ("bo", "L0_attn_b_out"),
    ("gam", "L0_attn_ln_gamma"), ("bet", "L0_attn_ln_beta"),
    ("wg", "L0_ffn_w_gate"), ("wu", "L0_ffn_w_up"), ("wd", "L0_ffn_w_down"),
]


def kernel(**inputs):
    x = np.ascontiguousarray(np.asarray(inputs["L0_attn_x"], dtype=np.float32))
    ws = {k: np.ascontiguousarray(np.asarray(inputs[n], dtype=np.float32))
          for k, n in _WNAMES}
    nc = build()
    in_maps = []
    for c in range(NCORES):
        m = {"x": x[c * BPC:(c + 1) * BPC]}
        m.update(ws)
        in_maps.append(m)
    res = run_bass_kernel_spmd(nc, in_maps, core_ids=list(range(NCORES)))
    out = np.concatenate([np.asarray(r["out"], dtype=np.float32)
                          for r in res.results], axis=0)
    return out
